# version 01
# baseline (speedup 1.0000x reference)
import numpy as np
import ml_dtypes
import concourse.bass as bass
import concourse.mybir as mybir
from concourse.bass_utils import run_bass_kernel_spmd

F32 = mybir.dt.float32
BF16 = mybir.dt.bfloat16
U8 = mybir.dt.uint8
AF = mybir.ActivationFunctionType
ALU = mybir.AluOpType
AX = mybir.AxisListType
BF = ml_dtypes.bfloat16

NCORES = 8
P = 128
D = 1024
B = 4
S = 4096
SL = S // NCORES
T = B * SL
DC = D // P
NEG = -1e9
EPS = 1e-5

NE = 64
FF = 512
FC = FF // P
NH = 16
HD = 64
WIN = 128
EXT = SL + 2 * WIN
KW = 3 * P
NTOK = T // P


def _f32(x):
    return np.ascontiguousarray(x, dtype=np.float32)



def host_prep(inp):
    x = np.asarray(inp["L0_attn_a_hidden_states"], np.float32)
    xT = np.ascontiguousarray(x.transpose(2, 0, 1))
    cache = np.asarray(inp["L0_attn_a_rope_cache"], np.float32)

    per_core = []
    i_of_f = (np.arange(P) % HD) // 2
    for c in range(NCORES):
        p0 = c * SL
        d = {}
        xe = np.zeros((D, B, EXT), np.float32)
        lo, hi = max(p0 - WIN, 0), min(p0 + SL + WIN, S)
        xe[:, :, lo - (p0 - WIN): hi - (p0 - WIN)] = xT[:, :, lo:hi]
        d["x_ext"] = xe
        pos = np.arange(p0 - WIN, p0 + SL + WIN) % 256
        d["ropeC"] = _f32(cache[pos][:, i_of_f, 0].T)
        d["ropeS"] = _f32(cache[pos][:, i_of_f, 1].T)
        m = np.empty((P, 4 * KW), np.float32)
        for qt in range(4):
            qpos = p0 + qt * 128 + np.arange(P)[:, None]
            kpos = p0 + qt * 128 - WIN + np.arange(KW)[None, :]
            ok = (kpos >= 0) & (kpos < S) & (np.abs(qpos - kpos) <= WIN)
            m[:, qt * KW:(qt + 1) * KW] = np.where(ok, 0.0, NEG)
        d["swmaskB"] = m.astype(BF)
        d["res_in"] = np.ascontiguousarray(
            xT[:, :, p0:p0 + SL].reshape(D, T))
        per_core.append(d)

    g = {}
    g["identB"] = np.eye(P, dtype=BF)
    g["ones_colB"] = np.ones((P, 1), dtype=BF)
    g["ones_rowB"] = np.ones((1, P), dtype=BF)
    g["cmaskB"] = np.ascontiguousarray(np.triu(np.ones((P, P), np.float32))).astype(BF)
    P128 = np.zeros((P, P), np.float32)
    for i in range(P // 2):
        P128[2 * i + 1, 2 * i] = -1.0
        P128[2 * i, 2 * i + 1] = 1.0
    g["pswapB"] = P128.astype(BF)

    g["Wqkv"] = _f32(inp["L0_attn_a_Wqkv_kernel"])
    g["bqkv"] = _f32(np.asarray(inp["L0_attn_a_Wqkv_bias"]).reshape(24, P).T)
    g["Wo0"] = _f32(inp["L0_attn_a_Wo_kernel"])
    g["bo0"] = _f32(np.asarray(inp["L0_attn_a_Wo_bias"]).reshape(DC, P).T)

    g["centT"] = _f32(np.asarray(inp["L0_ffn_centroids"]).T)
    g["ebias"] = _f32(np.asarray(inp["L0_ffn_expert_biases"]).reshape(1, NE))
    g["Wsu"] = _f32(inp["L0_ffn_Ws_up"])
    g["Wsd"] = _f32(inp["L0_ffn_Ws_down"])
    g["Wu"] = _f32(inp["L0_ffn_W_up"])
    g["Wd"] = _f32(inp["L0_ffn_W_down"])

    g["WzT"] = _f32(np.asarray(inp["L0_attn_b_Wz"]).T)
    g["bz"] = _f32(np.asarray(inp["L0_attn_b_bz"]).reshape(DC, P).T)
    g["WhT"] = _f32(np.asarray(inp["L0_attn_b_Wh"]).T)
    g["bh"] = _f32(np.asarray(inp["L0_attn_b_bh"]).reshape(DC, P).T)

    for nm in ("q", "k", "v", "o"):
        g[f"A1W{nm}T"] = _f32(np.asarray(inp[f"L1_attn_a_W{nm}"]).T)
        g[f"A1b{nm}"] = _f32(np.asarray(inp[f"L1_attn_a_b{nm}"]).reshape(DC, P).T)
        g[f"B1W{nm}T"] = _f32(np.asarray(inp[f"L1_attn_b_W{nm}"]).T)
        g[f"B1b{nm}"] = _f32(np.asarray(inp[f"L1_attn_b_b{nm}"]).reshape(DC, P).T)
    g["B1W1T"] = _f32(np.asarray(inp["L1_attn_b_W1"]).T)
    g["B1b1"] = _f32(np.asarray(inp["L1_attn_b_b1"]).reshape(DC, P).T)
    g["B1W2T"] = _f32(np.asarray(inp["L1_attn_b_W2"]).T)
    g["B1b2c"] = _f32(np.asarray(inp["L1_attn_b_b2"]).reshape(NE, 1))
    g["B1sigT"] = _f32(np.asarray(inp["L1_attn_b_head_signatures"]).T)

    g["F1WrT"] = _f32(np.asarray(inp["L1_ffn_W_r"]).T)
    g["F1WkT"] = _f32(np.asarray(inp["L1_ffn_W_k"]).T)
    g["F1WvT"] = _f32(np.asarray(inp["L1_ffn_W_v"]).T)
    g["F1mur"] = _f32(np.asarray(inp["L1_ffn_mu_r"]).reshape(DC, P).T)
    g["F1muk"] = _f32(np.asarray(inp["L1_ffn_mu_k"]).reshape(DC, P).T)

    hs = np.zeros((P, DC * NH), np.float32)
    hb = np.zeros((NH, DC * P), np.float32)
    for cc in range(DC):
        for p in range(P):
            h = (cc * P + p) // HD
            hs[p, cc * NH + h] = 1.0
            hb[h, cc * P + p] = 1.0
    g["hselB"] = hs.astype(BF)
    g["hbselB"] = hb.astype(BF)
    return g, per_core



class Ctx:
    def __init__(self, nc, block):
        self.nc = nc
        self.block = block
        self.sem = {}
        self.cnt = {}
        self.gap = {}

    def s(self, name):
        if name not in self.sem:
            self.sem[name] = self.nc.alloc_semaphore(name)
            self.cnt[name] = 0
        return self.sem[name]

    def n(self, name):
        self.s(name)
        return self.cnt[name]

    def inc(self, instr, name, k=1):
        instr.then_inc(self.s(name), k)
        self.cnt[name] += k
        return self.cnt[name]

    def w(self, eng, name, val):
        eng.wait_ge(self.s(name), val)


def alloc_common(nc):
    a = {}
    a["res"] = nc.alloc_sbuf_tensor("res", [P, DC * T], F32)
    a["b1"] = nc.alloc_sbuf_tensor("b1", [P, 24576], BF16)
    a["b2"] = nc.alloc_sbuf_tensor("b2", [P, DC * T], BF16)
    a["f1"] = nc.alloc_sbuf_tensor("f1", [P, 11264], F32)
    a["ident"] = nc.alloc_sbuf_tensor("ident", [P, P], BF16)
    a["ones_col"] = nc.alloc_sbuf_tensor("ones_col", [P, 1], BF16)
    a["ones_row"] = nc.alloc_sbuf_tensor("ones_row", [1, P], BF16)
    a["b3"] = nc.alloc_sbuf_tensor("b3", [P, 8192], BF16)
    a["cols"] = nc.alloc_sbuf_tensor("cols", [P, 16], F32)
    a["colsb"] = nc.alloc_sbuf_tensor("colsb", [P, 64], BF16)
    a["biasq"] = nc.alloc_sbuf_tensor("biasq", [P, 64], F32)
    a["pb"] = [nc.alloc_psum_tensor(f"pb{i}", [P, 512], F32) for i in range(6)]
    a["pt"] = [nc.alloc_psum_tensor(f"pt{i}", [P, 1024], BF16) for i in range(2)]
    a["ptf"] = [t[:].bitcast(F32) for t in a["pt"]]
    return a


def emit_load_consts(c, a):
    @c.block.sync
    def _(sync):
        c.inc(sync.dma_start(out=a["ident"][:], in_=c.gap["identB"]), "sld", 16)
        c.inc(sync.dma_start(out=a["ones_col"][:], in_=c.gap["ones_colB"]), "sld", 16)
        c.inc(sync.dma_start(out=a["ones_row"][:], in_=c.gap["ones_rowB"]), "sld", 16)


def emit_load_res(c, a, res_ap):
    @c.block.sync
    def _(sync):
        c.inc(sync.dma_start(out=a["res"][:].rearrange("p (cc t) -> p cc t", cc=DC),
                             in_=res_ap.rearrange("(cc p) t -> p cc t", p=P)),
              "sld", 16)


def emit_store_res(c, a, out_ap, wait_lanes):
    @c.block.sync
    def _(sync):
        for name, val in wait_lanes:
            c.w(sync, name, val)
        c.inc(sync.dma_start(out=out_ap.rearrange("(cc p) t -> p cc t", p=P),
                             in_=a["res"][:].rearrange("p (cc t) -> p cc t", cc=DC)),
              "sst", 16)
        c.w(sync, "sst", c.n("sst"))



def emit_ln(c, a, dst, ncols=T, src=None, src_stride=None, src_off=0,
            nchunk=DC, sq_arena=None, rows=None, rows_bf=None, pre=(), pre_t=()):
    if src is None:
        src = a["res"][:]
    stride = src_stride if src_stride is not None else ncols
    tiles = [(o, min(512, ncols - o)) for o in range(0, ncols, 512)]
    NT = len(tiles)
    assert NT <= 4
    if sq_arena is None:
        b1w = a["b1"].shape[1]
        sq_arena = a["b1"][:, b1w - 2 * ncols:]
    sq = [sq_arena[:, 0:ncols], sq_arena[:, ncols:2 * ncols]]
    if rows is None:
        rows = a["f1"][0:1, 0:3 * ncols]
    if rows_bf is None:
        b1w = a["b1"].shape[1]
        rows_bf = a["b1"][0:1, b1w - 2 * ncols - 2 * ncols: b1w - 2 * ncols]
    psum_s = [a["pb"][i] for i in range(4)]
    psum_q = [a["pb"][4], a["pb"][5], a["ptf"][0], a["ptf"][1]]

    v0 = c.n("vev")
    s0 = c.n("sev")
    t0 = c.n("tmm")
    sl0 = c.n("sld")

    def seqv(v, ins):
        mark = c.inc(ins, "vsq", 1)
        c.w(v, "vsq", mark)

    @c.block.vector
    def _(v):
        for ln, lv in pre:
            c.w(v, ln, lv)
        ins = None
        for ch in range(nchunk):
            ins = v.tensor_copy(dst[:, ch * ncols:(ch + 1) * ncols],
                                src[:, ch * stride + src_off: ch * stride + src_off + ncols])
        c.inc(ins, "vev", 1)

    @c.block.scalar
    def _(s):
        for ln, lv in pre:
            c.w(s, ln, lv)
        for ch in range(nchunk):
            if ch >= 2:
                c.w(s, "tmm", t0 + ch - 1)
            c.inc(s.activation(sq[ch % 2][:, 0:ncols],
                               src[:, ch * stride + src_off: ch * stride + src_off + ncols],
                               AF.Square), "sev", 1)

    @c.block.tensor
    def _(t):
        for ln, lv in pre_t:
            c.w(t, ln, lv)
        c.w(t, "vev", v0 + 1)
        c.w(t, "sld", sl0)
        for ch in range(nchunk):
            c.w(t, "sev", s0 + ch + 1)
            for ti, (off, wdt) in enumerate(tiles):
                t.matmul(psum_s[ti][0:1, 0:wdt], a["ones_col"][:],
                         dst[:, ch * ncols + off: ch * ncols + off + wdt],
                         start=(ch == 0), stop=(ch == nchunk - 1))
                ins = t.matmul(psum_q[ti][0:1, 0:wdt], a["ones_col"][:],
                               sq[ch % 2][:, off:off + wdt],
                               start=(ch == 0), stop=(ch == nchunk - 1))
            c.inc(ins, "tmm", 1)

    @c.block.vector
    def _(v):
        c.w(v, "tmm", t0 + nchunk)
        inv = 1.0 / (nchunk * P)
        ins = None
        for ti, (off, wdt) in enumerate(tiles):
            v.tensor_scalar(rows[:, off:off + wdt],
                            psum_s[ti][0:1, 0:wdt], inv, None, ALU.mult)
            ins = v.tensor_scalar(rows[:, ncols + off: ncols + off + wdt],
                                  psum_q[ti][0:1, 0:wdt], inv, None, ALU.mult)
        seqv(v, ins)
        seqv(v, v.tensor_mul(rows[:, 2 * ncols:3 * ncols], rows[:, 0:ncols],
                             rows[:, 0:ncols]))
        seqv(v, v.tensor_sub(rows[:, ncols:2 * ncols], rows[:, ncols:2 * ncols],
                             rows[:, 2 * ncols:3 * ncols]))
        c.inc(v.tensor_scalar(rows[:, ncols:2 * ncols], rows[:, ncols:2 * ncols],
                              EPS, None, ALU.add), "vev", 1)

    @c.block.scalar
    def _(s):
        c.w(s, "vev", v0 + 2)
        mark = c.inc(s.activation(rows[:, ncols:2 * ncols],
                                  rows[:, ncols:2 * ncols], AF.Ln), "ssq", 1)
        c.w(s, "ssq", mark)
        c.inc(s.activation(rows[:, ncols:2 * ncols],
                           rows[:, ncols:2 * ncols], AF.Exp, scale=-0.5),
              "sev", 1)

    @c.block.vector
    def _(v):
        c.w(v, "sev", s0 + nchunk + 1)
        v.tensor_copy(rows_bf[:, 0:ncols], rows[:, ncols:2 * ncols])
        seqv(v, v.tensor_mul(rows[:, 2 * ncols:3 * ncols], rows[:, 0:ncols],
                             rows[:, ncols:2 * ncols]))
        c.inc(v.tensor_copy(rows_bf[:, ncols:2 * ncols],
                            rows[:, 2 * ncols:3 * ncols]), "vev", 1)

    t1 = c.n("tmm")
    v1 = c.n("vev")

    @c.block.tensor
    def _(t):
        c.w(t, "vev", v1)
        for ti, (off, wdt) in enumerate(tiles):
            if ti >= 2:
                c.w(t, "vev", v1 + ti - 1)
            pr = a["pb"][2 * (ti % 2)]
            pm = a["pb"][2 * (ti % 2) + 1]
            t.matmul(pr[:, 0:wdt], a["ones_row"][:],
                     rows_bf[:, off:off + wdt], start=True, stop=True)
            ins = t.matmul(pm[:, 0:wdt], a["ones_row"][:],
                           rows_bf[:, ncols + off: ncols + off + wdt],
                           start=True, stop=True)
            c.inc(ins, "tmm", 1)

    @c.block.vector
    def _(v):
        for ti, (off, wdt) in enumerate(tiles):
            c.w(v, "tmm", t1 + ti + 1)
            pr = a["pb"][2 * (ti % 2)]
            pm = a["pb"][2 * (ti % 2) + 1]
            ins = None
            for ch in range(nchunk):
                o = ch * ncols + off
                so = ch * stride + src_off + off
                seqv(v, v.tensor_mul(dst[:, o:o + wdt], src[:, so:so + wdt],
                                     pr[:, 0:wdt]))
                ins = v.tensor_sub(dst[:, o:o + wdt], dst[:, o:o + wdt],
                                   pm[:, 0:wdt])
            c.inc(ins, "vev", 1)



def emit_moe(c, a, wu_ap, wd_ap, wsu_ap, wsd_ap, centT_ap, ebias_ap,
             router_only=False):
    xn = a["b2"][:]
    b1 = a["b1"][:]
    wu_b = [b1[:, 0:4096], b1[:, 4096:8192]]
    wd_b = [b1[:, 8192:12288], b1[:, 12288:16384]]
    htil = b1[:, 16384:18432]
    cwT = b1[:, 18432:20480]
    centT_sb = b1[:, 20480:20480 + DC * NE]
    cw_tm = b1[:, 20992:20992 + NTOK * NE]
    cwb = b1[:, 22016:22528]
    p_up = [a["pb"][0], a["pb"][1]]
    p_dn = [a["pb"][2], a["pb"][3], a["pb"][4], a["pb"][5]]
    p_bc = a["ptf"][0]
    p_rt = a["ptf"][1]
    p_tr = a["pt"][1]

    TT = 512
    NTILE = T // TT
    MMI = 1 + FC + DC

    cwrow = a["f1"][0:1, 2304:2304 + T].bitcast(BF16)
    aff = a["f1"][:, 0:NTOK * NE]
    sig = a["f1"][:, 1024:1024 + NTOK * NE]
    rep = a["f1"][:, 2048:2048 + NE]
    msk = a["f1"][:, 2112:2112 + NE]
    wts = a["f1"][:, 2176:2176 + NE]
    biased = a["f1"][:, 2240:2240 + NE]
    max8 = a["cols"][:, 8:16]
    zcol = a["cols"][:, 0:1]
    rzcol = a["cols"][:, 1:2]

    v0 = c.n("vev"); t0 = c.n("tmm"); s0 = c.n("sev"); r0 = c.n("vrt")
    g0 = c.n("gld")
    wa0 = c.n("wsA"); wb0 = c.n("wsB"); wf0 = c.n("wfree")
    ca0 = c.n("cwA"); cb0 = c.n("cwB")
    NEXP = NE + 1

    def wsrc(e):
        if e == 0:
            return (wsu_ap.rearrange("(cc p) f -> p cc f", p=P),
                    wsd_ap.rearrange("(cc p) dd -> p cc dd", p=P))
        return (wu_ap[e - 1].rearrange("(cc p) f -> p cc f", p=P),
                wd_ap[e - 1].rearrange("(cc p) dd -> p cc dd", p=P))

    @c.block.gpsimd
    def _(g):
        for e in range(0 if router_only else 2):
            su, sd = wsrc(e)
            wsem = "wsA" if e % 2 == 0 else "wsB"
            c.inc(g.dma_start(out=wu_b[e % 2].rearrange("p (cc f) -> p cc f", cc=DC),
                              in_=su), wsem, 16)
            c.inc(g.dma_start(out=wd_b[e % 2].rearrange("p (cc dd) -> p cc dd", cc=FC),
                              in_=sd), wsem, 16)
        c.w(g, "vev", v0)
        c.inc(g.dma_start(out=centT_sb.rearrange("p (cc e) -> p cc e", cc=DC),
                          in_=centT_ap.rearrange("(cc p) e -> p cc e", p=P)),
              "gld", 16)
        c.inc(g.dma_start(out=a["colsb"][0:1, 0:NE], in_=ebias_ap), "gld", 16)
        for e in range(NEXP, NEXP) if router_only else range(2, NEXP):
            c.w(g, "wfree", wf0 + (e - 1) * 16)
            su, sd = wsrc(e)
            wsem = "wsA" if e % 2 == 0 else "wsB"
            c.inc(g.dma_start(out=wu_b[e % 2].rearrange("p (cc f) -> p cc f", cc=DC),
                              in_=su), wsem, 16)
            c.inc(g.dma_start(out=wd_b[e % 2].rearrange("p (cc dd) -> p cc dd", cc=FC),
                              in_=sd), wsem, 16)

    @c.block.tensor
    def _(t):
        c.w(t, "gld", g0 + 32)
        c.w(t, "vev", v0)
        ins = t.matmul(p_bc[:, 0:NE], a["ones_row"][:], a["colsb"][0:1, 0:NE],
                       start=True, stop=True)
        c.inc(ins, "tmm", 1)

    @c.block.vector
    def _(v):
        c.w(v, "tmm", t0 + 1)
        c.inc(v.tensor_copy(a["biasq"][:, 0:NE], p_bc[:, 0:NE]), "vrt", 1)

    @c.block.tensor
    def _(t):
        for j in range(NTOK):
            c.w(t, "vrt", r0 + 1 + 2 * j)
            for ch in range(DC):
                ins = t.matmul(p_rt[:, 0:NE],
                               xn[:, ch * T + j * P: ch * T + (j + 1) * P],
                               centT_sb[:, ch * NE:(ch + 1) * NE],
                               start=(ch == 0), stop=(ch == DC - 1))
            c.inc(ins, "tmm", 1)

    @c.block.vector
    def _(v):
        for j in range(NTOK):
            c.w(v, "tmm", t0 + 2 + j)
            mark = c.inc(v.tensor_copy(aff[:, j * NE:(j + 1) * NE],
                                       p_rt[:, 0:NE]), "vrt", 1)
            c.w(v, "vrt", mark)
            mark = c.inc(v.tensor_add(biased[:, 0:NE], aff[:, j * NE:(j + 1) * NE],
                                      a["biasq"][:, 0:NE]), "vsq", 1)
            c.w(v, "vsq", mark)
            c.inc(v.max(max8[:, 0:8], biased[:, 0:NE]), "vrt", 1)
            c.w(v, "sev", s0 + j + 1)
            mark = c.inc(v.match_replace(rep[:, 0:NE], max8[:, 0:8],
                                         biased[:, 0:NE], -1e30), "vsq", 1)
            c.w(v, "vsq", mark)
            mark = c.inc(v.tensor_tensor(msk[:, 0:NE], biased[:, 0:NE],
                                         rep[:, 0:NE], ALU.is_gt), "vsq", 1)
            c.w(v, "vsq", mark)
            mark = c.inc(v.tensor_mul(wts[:, 0:NE], sig[:, j * NE:(j + 1) * NE],
                                      msk[:, 0:NE]), "vsq", 1)
            c.w(v, "vsq", mark)
            mark = c.inc(v.reduce_sum(zcol[:, 0:1], wts[:, 0:NE], AX.X), "vsq", 1)
            c.w(v, "vsq", mark)
            mark = c.inc(v.tensor_scalar(zcol[:, 0:1], zcol[:, 0:1], 1e-8,
                                         None, ALU.add), "vsq", 1)
            c.w(v, "vsq", mark)
            mark = c.inc(v.reciprocal(rzcol[:, 0:1], zcol[:, 0:1]), "vsq", 1)
            c.w(v, "vsq", mark)
            c.inc(v.tensor_scalar(cw_tm[:, j * NE:(j + 1) * NE], wts[:, 0:NE],
                                  rzcol[:, 0:1], None, ALU.mult), "vev", 1)

    @c.block.scalar
    def _(s):
        for j in range(NTOK):
            c.w(s, "vrt", r0 + 3 + 2 * j)
            c.inc(s.activation(sig[:, j * NE:(j + 1) * NE],
                               aff[:, j * NE:(j + 1) * NE], AF.Sigmoid),
                  "sev", 1)

    t1 = c.n("tmm")
    r1 = c.n("vrt")
    all_aff_done = r0 + 1 + 2 * NTOK

    @c.block.tensor
    def _(t):
        c.w(t, "vrt", all_aff_done)
        for j in range(NTOK):
            c.w(t, "vev", v0 + j + 1)
            if j >= 2:
                c.w(t, "vrt", r1 + j - 1)
            ins = t.transpose(a["pt"][j % 2][0:NE, 0:128],
                              cw_tm[:, j * NE:(j + 1) * NE], a["ident"][:])
            c.inc(ins, "tmm", 1)

    @c.block.vector
    def _(v):
        for j in range(NTOK):
            c.w(v, "tmm", t1 + j + 1)
            c.inc(v.tensor_copy(cwT[0:NE, j * P:(j + 1) * P],
                                a["pt"][j % 2][0:NE, 0:128]),
                  "vrt", 1)

    router_done_vrt = c.n("vrt")
    if router_only:
        return

    @c.block.sync
    def _(sync):
        for e in range(1, NEXP):
            slot = (e - 1) % 2
            csem = "cwA" if slot == 0 else "cwB"
            if e == 1:
                c.w(sync, "vrt", router_done_vrt)
            if e >= 3:
                c.w(sync, "wfree", wf0 + (e - 1) * 16)
            c.inc(sync.dma_start(out=cwrow[0:1, slot * T:(slot + 1) * T],
                                 in_=cwT[e - 1:e, 0:T]), csem, 16)

    t2 = c.n("tmm"); s2 = c.n("sev"); v2 = c.n("vev"); d0 = c.n("vdn")

    @c.block.tensor
    def _(t):
        for e in range(NEXP):
            buf = e % 2
            wsem = "wsA" if buf == 0 else "wsB"
            wbase = wa0 if buf == 0 else wb0
            c.w(t, wsem, wbase + (e // 2) * 32 + 32)
            if e >= 1:
                slot = (e - 1) % 2
                csem = "cwA" if slot == 0 else "cwB"
                cbase = ca0 if slot == 0 else cb0
                c.w(t, csem, cbase + ((e - 1) // 2) * 16 + 16)
            for tt in range(NTILE):
                it = e * NTILE + tt
                if e >= 1:
                    if it > 0:
                        c.w(t, "vev", v2 + (it - 1) * (1 + FC) + 1)
                    slot = (e - 1) % 2
                    ins = t.matmul(p_bc[:, 0:TT], a["ones_row"][:],
                                   cwrow[0:1, slot * T + tt * TT:
                                         slot * T + (tt + 1) * TT],
                                   start=True, stop=True)
                    c.inc(ins, "tmm", 1)
                    skip_bc = 0
                else:
                    skip_bc = 1
                for f in range(FC):
                    gidx = it * FC + f
                    if gidx >= 2:
                        c.w(t, "sev", s2 + gidx - 1)
                    pu = p_up[f % 2]
                    for ch in range(DC):
                        ins = t.matmul(pu[:, 0:TT],
                                       wu_b[buf][:, ch * FF + f * P: ch * FF + (f + 1) * P],
                                       xn[:, ch * T + tt * TT: ch * T + (tt + 1) * TT],
                                       start=(ch == 0), stop=(ch == DC - 1))
                    c.inc(ins, "tmm", 1 + (skip_bc if f == 0 else 0))
                c.w(t, "vev", v2 + it * (1 + FC) + 1 + FC)
                for dd in range(DC):
                    gd = it * DC + dd
                    if gd >= 4:
                        c.w(t, "vdn", d0 + gd - 3)
                    pd = p_dn[dd % 4]
                    for f in range(FC):
                        ins = t.matmul(pd[:, 0:TT],
                                       wd_b[buf][:, f * D + dd * P: f * D + (dd + 1) * P],
                                       htil[:, f * TT:(f + 1) * TT],
                                       start=(f == 0), stop=(f == FC - 1))
                    c.inc(ins, "tmm", 1)

    @c.block.scalar
    def _(s):
        for e in range(NEXP):
            for tt in range(NTILE):
                it = e * NTILE + tt
                for f in range(FC):
                    c.w(s, "tmm", t2 + it * MMI + 1 + f + 1)
                    c.inc(s.activation(htil[:, f * TT:(f + 1) * TT],
                                       p_up[f % 2][:, 0:TT], AF.Silu),
                          "sev", 1)

    @c.block.vector
    def _(v):
        for e in range(NEXP):
            for tt in range(NTILE):
                it = e * NTILE + tt
                if e >= 1:
                    c.w(v, "tmm", t2 + it * MMI + 1)
                    mark = c.inc(v.tensor_copy(cwb[:, 0:TT], p_bc[:, 0:TT]),
                                 "vev", 1)
                    c.w(v, "vev", mark)
                else:
                    c.inc(v.engine_nop(), "vev", 1)
                for f in range(FC):
                    c.w(v, "sev", s2 + it * FC + f + 1)
                    if e >= 1:
                        ins = v.tensor_mul(htil[:, f * TT:(f + 1) * TT],
                                           htil[:, f * TT:(f + 1) * TT],
                                           cwb[:, 0:TT])
                    else:
                        ins = v.engine_nop()
                    c.inc(ins, "vev", 1)
                for dd in range(DC):
                    gd = it * DC + dd
                    c.w(v, "tmm", t2 + it * MMI + 1 + FC + dd + 1)
                    if gd >= 4:
                        c.w(v, "vdn", d0 + gd - 3)
                    rsl = a["res"][:, dd * T + tt * TT: dd * T + (tt + 1) * TT]
                    c.inc(v.tensor_add(rsl, rsl, p_dn[dd % 4][:, 0:TT]),
                          "vdn", 1)
            c.inc(v.engine_nop(), "wfree", 16)




def emit_attn_a(c, a, aps):
    b1 = a["b1"][:]; b2 = a["b2"][:]; f1 = a["f1"][:]
    qkv = b1[:, 0:18432]
    piece = [b1[:, 18432:19456], b1[:, 19456:20480]]
    sq_ar = b1[:, 20480:22016]
    attn_fm = b1[:, 20480:24576]
    xe = f1[:, 0:6144]
    ropeC = f1[:, 6144:6912]; ropeS = f1[:, 6912:7680]
    rows = a["f1"][0:1, 7680:9984]
    xn = b2[:, 0:6144]
    swm = b2[:, 6144:7680]
    probs = b2[:, 7680:8064]
    probsT = b2[:, 8064:8448]
    vtm = b2[:, 8448:14592]
    rt1 = b2[:, 14592:15360]
    rt2 = b2[:, 15360:16128]
    pb = a["pb"]; pt = a["pt"]
    bq = a["biasq"]

    E3 = 768
    TILES = ((0, 512), (512, 256))

    @c.block.sync
    def _(sync):
        c.inc(sync.dma_start(out=ropeC, in_=aps["ropeC"]), "sld", 16)
        c.inc(sync.dma_start(out=ropeS, in_=aps["ropeS"]), "sld", 16)
        c.inc(sync.dma_start(out=swm, in_=aps["swmaskB"]), "sld", 16)
        c.inc(sync.dma_start(out=bq[:, 0:24], in_=aps["bqkv"]), "sld", 16)
        c.inc(sync.dma_start(out=bq[:, 24:32], in_=aps["bo0"]), "sld", 16)
    sld_tabs = c.n("sld")

    for b in range(B):
        prev_vev = c.n("vev")

        @c.block.sync
        def _(sync):
            if b > 0:
                c.w(sync, "vev", prev_vev)
            c.inc(sync.dma_start(
                out=xe.rearrange("p (cc t) -> p cc t", cc=DC),
                in_=aps["x_ext"][:, b, :].rearrange("(cc p) t -> p cc t", p=P)),
                "sld", 16)

        emit_ln(c, a, xn, ncols=E3, src=xe, src_stride=E3, nchunk=DC,
                sq_arena=sq_ar, rows=rows,
                rows_bf=b1[0:1, 23040:24576],
                pre=[("sld", c.n("sld")), ("tmm", c.n("tmm"))],
                pre_t=[("vev", c.n("vev"))])

        ln_end = c.n("vev")
        t_q0 = c.n("tmm"); s_q0 = c.n("sev")
        wb = {0: c.n("wsA"), 1: c.n("wsB")}

        @c.block.gpsimd
        def _(g):
            for oc in range(24):
                lane = "wsA" if oc % 2 == 0 else "wsB"
                if oc < 2:
                    c.w(g, "tmm", t_q0)
                else:
                    c.w(g, "tmm", t_q0 + 2 * (oc - 1))
                c.inc(g.dma_start(
                    out=piece[oc % 2].rearrange("p (cc f) -> p cc f", cc=DC),
                    in_=aps["Wqkv"].rearrange("(cc p) f -> p cc f", p=P)
                        [:, :, oc * 128:(oc + 1) * 128]), lane, 16)

        @c.block.tensor
        def _(t):
            c.w(t, "vev", ln_end)
            for oc in range(24):
                lane = "wsA" if oc % 2 == 0 else "wsB"
                c.w(t, lane, wb[oc % 2] + (oc // 2) * 16 + 16)
                for u, (off, wdt) in enumerate(TILES):
                    gi = 2 * oc + u
                    if gi >= 2:
                        c.w(t, "sev", s_q0 + gi - 1)
                    for cc in range(DC):
                        ins = t.matmul(pb[gi % 2][:, 0:wdt],
                                       piece[oc % 2][:, cc * 128:(cc + 1) * 128],
                                       xn[:, cc * E3 + off: cc * E3 + off + wdt],
                                       start=(cc == 0), stop=(cc == DC - 1))
                    c.inc(ins, "tmm", 1)

        @c.block.scalar
        def _(s):
            for oc in range(24):
                for u, (off, wdt) in enumerate(TILES):
                    gi = 2 * oc + u
                    c.w(s, "tmm", t_q0 + gi + 1)
                    c.inc(s.activation(qkv[:, oc * E3 + off: oc * E3 + off + wdt],
                                       pb[gi % 2][:, 0:wdt], AF.Identity,
                                       bias=bq[:, oc:oc + 1]), "sev", 1)

        t_r0 = c.n("tmm"); v_r0 = c.n("vev"); r_r0 = c.n("vrt")

        @c.block.tensor
        def _(t):
            for ch in range(16):
                c.w(t, "sev", s_q0 + 2 * ch + 2)
                if ch > 0:
                    c.w(t, "vrt", r_r0 + ch)
                t.matmul(pb[2][:, 0:512], a["pswap"][:],
                         qkv[:, ch * E3: ch * E3 + 512], start=True, stop=True)
                ins = t.matmul(pb[3][:, 0:256], a["pswap"][:],
                               qkv[:, ch * E3 + 512: ch * E3 + 768],
                               start=True, stop=True)
                c.inc(ins, "tmm", 1)

        @c.block.vector
        def _(v):
            for ch in range(16):
                c.w(v, "tmm", t_r0 + ch + 1)
                if ch > 0:
                    c.w(v, "vev", v_r0 + ch)
                v.tensor_mul(rt1[:, 0:512], pb[2][:, 0:512], ropeS[:, 0:512])
                mark = c.inc(v.tensor_mul(rt1[:, 512:768], pb[3][:, 0:256],
                                          ropeS[:, 512:768]), "vrt", 1)
                c.w(v, "vrt", mark)
                c.w(v, "vsq", c.inc(v.tensor_mul(
                    rt2[:, 0:768], qkv[:, ch * E3: ch * E3 + 768],
                    ropeC[:, 0:768]), "vsq", 1))
                c.inc(v.tensor_add(qkv[:, ch * E3: ch * E3 + 768],
                                   rt1[:, 0:768], rt2[:, 0:768]), "vev", 1)
        rope_done = c.n("vev")

        t_v0 = c.n("tmm"); r_v0 = c.n("vrt")

        @c.block.tensor
        def _(t):
            c.w(t, "vev", rope_done)
            c.w(t, "sev", s_q0 + 48)
            for h in range(NH):
                vb = 64 * (h % 2)
                vch = 16 + h // 2
                for kt in range(6):
                    gi = h * 6 + kt
                    if gi >= 2:
                        c.w(t, "vrt", r_v0 + gi - 1)
                    ins = t.transpose(pt[gi % 2][0:128, 0:64],
                                      qkv[vb:vb + 64,
                                          vch * E3 + kt * 128: vch * E3 + (kt + 1) * 128],
                                      a["ident"][vb:vb + 64, vb:vb + 64])
                    c.inc(ins, "tmm", 1)

        @c.block.vector
        def _(v):
            for h in range(NH):
                for kt in range(6):
                    gi = h * 6 + kt
                    c.w(v, "tmm", t_v0 + gi + 1)
                    c.inc(v.tensor_copy(vtm[:, (h * 6 + kt) * 64:(h * 6 + kt + 1) * 64],
                                        pt[gi % 2][0:128, 0:64]), "vrt", 1)
        vtm_done = c.n("vrt")

        t_a0 = c.n("tmm"); v_a0 = c.n("vev"); s_a0 = c.n("sev"); r_a0 = c.n("vrt")
        NIDX = NH * 4

        @c.block.tensor
        def _(t):
            for idx in range(NIDX):
                h, qt = idx // 4, idx % 4
                qb = 64 * (h % 2)
                qc, kc = h // 2, 8 + h // 2
                qoff = WIN + qt * 128
                kloc = qt * 128
                if idx == 0:
                    c.w(t, "vrt", vtm_done)
                if idx >= 2:
                    c.w(t, "sev", s_a0 + idx - 1)
                ins = t.matmul(pb[4 + idx % 2][:, 0:KW],
                               qkv[qb:qb + 64, qc * E3 + qoff: qc * E3 + qoff + 128],
                               qkv[qb:qb + 64, kc * E3 + kloc: kc * E3 + kloc + KW],
                               start=True, stop=True)
                c.inc(ins, "tmm", 1)
                c.w(t, "vev", v_a0 + idx * 3 + 2)
                if idx >= 1:
                    c.w(t, "vrt", r_a0 + idx * 3)
                for k3 in range(3):
                    if k3 == 2:
                        c.w(t, "vrt", r_a0 + idx * 3 + 1)
                    ins = t.transpose(pt[k3 % 2][0:128, 0:128],
                                      probs[:, k3 * 128:(k3 + 1) * 128],
                                      a["ident"][:])
                    c.inc(ins, "tmm", 1)
                c.w(t, "vrt", r_a0 + idx * 3 + 3)
                if idx >= 2:
                    c.w(t, "vev", v_a0 + (idx - 2) * 3 + 3)
                for k3 in range(3):
                    ins = t.matmul(pb[2 + idx % 2][0:64, 0:128],
                                   vtm[:, (h * 6 + qt + k3) * 64:(h * 6 + qt + k3 + 1) * 64],
                                   probsT[:, k3 * 128:(k3 + 1) * 128],
                                   start=(k3 == 0), stop=(k3 == 2))
                c.inc(ins, "tmm", 1)

        @c.block.vector
        def _(v):
            for idx in range(NIDX):
                h, qt = idx // 4, idx % 4
                c.w(v, "tmm", t_a0 + idx * 5 + 1)
                psc = pb[4 + idx % 2]
                mark = c.inc(v.tensor_add(psc[:, 0:KW], psc[:, 0:KW],
                                          swm[:, qt * KW:(qt + 1) * KW]), "vsq", 1)
                c.w(v, "vsq", mark)
                mark = c.inc(v.reduce_max(a["cols"][:, 2:3], psc[:, 0:KW], AX.X),
                             "vsq", 1)
                c.w(v, "vsq", mark)
                c.inc(v.tensor_scalar(a["cols"][:, 3:4], a["cols"][:, 2:3],
                                      -0.125, None, ALU.mult), "vev", 1)
                c.w(v, "sev", s_a0 + idx + 1)
                mark = c.inc(v.reciprocal(a["cols"][:, 5:6], a["cols"][:, 4:5]),
                             "vsq", 1)
                c.w(v, "vsq", mark)
                c.inc(v.tensor_scalar(probs[:, 0:KW], probs[:, 0:KW],
                                      a["cols"][:, 5:6], None, ALU.mult),
                      "vev", 1)
                for k3 in range(3):
                    c.w(v, "tmm", t_a0 + idx * 5 + 2 + k3)
                    c.inc(v.tensor_copy(probsT[:, k3 * 128:(k3 + 1) * 128],
                                        pt[k3 % 2][0:128, 0:128]), "vrt", 1)
                c.w(v, "tmm", t_a0 + idx * 5 + 5)
                c.inc(v.tensor_copy(
                    attn_fm[64 * (h % 2):64 * (h % 2) + 64,
                            (h // 2) * 512 + qt * 128:(h // 2) * 512 + (qt + 1) * 128],
                    pb[2 + idx % 2][0:64, 0:128]), "vev", 1)

        @c.block.scalar
        def _(s):
            for idx in range(NIDX):
                c.w(s, "vev", v_a0 + idx * 3 + 1)
                if idx >= 1:
                    c.w(s, "tmm", t_a0 + (idx - 1) * 5 + 4)
                c.inc(s.activation(probs[:, 0:KW], pb[4 + idx % 2][:, 0:KW],
                                   AF.Exp, bias=a["cols"][:, 3:4], scale=0.125,
                                   accum_out=a["cols"][:, 4:5]), "sev", 1)
        attn_end = c.n("vev")

        t_w0 = c.n("tmm"); v_w0 = c.n("vev")
        wb2 = {0: c.n("wsA"), 1: c.n("wsB")}

        @c.block.gpsimd
        def _(g):
            for oc in range(DC):
                lane = "wsA" if oc % 2 == 0 else "wsB"
                if oc < 2:
                    c.w(g, "tmm", t_w0)
                else:
                    c.w(g, "tmm", t_w0 + oc - 1)
                c.inc(g.dma_start(
                    out=piece[oc % 2].rearrange("p (cc f) -> p cc f", cc=DC),
                    in_=aps["Wo0"].rearrange("(cc p) f -> p cc f", p=P)
                        [:, :, oc * 128:(oc + 1) * 128]), lane, 16)

        @c.block.tensor
        def _(t):
            for oc in range(DC):
                lane = "wsA" if oc % 2 == 0 else "wsB"
                c.w(t, lane, wb2[oc % 2] + (oc // 2) * 16 + 16)
                if oc == 0:
                    c.w(t, "vev", attn_end)
                if oc >= 2:
                    c.w(t, "vev", v_w0 + oc - 1)
                for cc in range(DC):
                    ins = t.matmul(pb[oc % 2][:, 0:512],
                                   piece[oc % 2][:, cc * 128:(cc + 1) * 128],
                                   attn_fm[:, cc * 512:(cc + 1) * 512],
                                   start=(cc == 0), stop=(cc == DC - 1))
                c.inc(ins, "tmm", 1)

        @c.block.vector
        def _(v):
            for oc in range(DC):
                c.w(v, "tmm", t_w0 + oc + 1)
                rsl = a["res"][:, oc * T + b * SL: oc * T + b * SL + SL]
                c.inc(v.scalar_tensor_tensor(rsl, pb[oc % 2][:, 0:512],
                                             bq[:, 24 + oc:25 + oc], rsl,
                                             ALU.add, ALU.add), "vev", 1)


def build_attn_a_launch():
    nc = bass.Bass("TRN2", target_bir_lowering=False)
    gap = {}
    gap["res_in"] = nc.dram_tensor("res_in", [D, T], F32, kind="ExternalInput").ap()
    gap["identB"] = nc.dram_tensor("identB", [P, P], BF16, kind="ExternalInput").ap()
    gap["ones_colB"] = nc.dram_tensor("ones_colB", [P, 1], BF16, kind="ExternalInput").ap()
    gap["ones_rowB"] = nc.dram_tensor("ones_rowB", [1, P], BF16, kind="ExternalInput").ap()
    gap["pswapB"] = nc.dram_tensor("pswapB", [P, P], BF16, kind="ExternalInput").ap()
    gap["x_ext"] = nc.dram_tensor("x_ext", [D, B, EXT], F32, kind="ExternalInput").ap()
    gap["ropeC"] = nc.dram_tensor("ropeC", [P, EXT], F32, kind="ExternalInput").ap()
    gap["ropeS"] = nc.dram_tensor("ropeS", [P, EXT], F32, kind="ExternalInput").ap()
    gap["swmaskB"] = nc.dram_tensor("swmaskB", [P, 4 * KW], BF16, kind="ExternalInput").ap()
    gap["Wqkv"] = nc.dram_tensor("Wqkv", [D, 3 * D], F32, kind="ExternalInput").ap()
    gap["bqkv"] = nc.dram_tensor("bqkv", [P, 24], F32, kind="ExternalInput").ap()
    gap["Wo0"] = nc.dram_tensor("Wo0", [D, D], F32, kind="ExternalInput").ap()
    gap["bo0"] = nc.dram_tensor("bo0", [P, DC], F32, kind="ExternalInput").ap()
    gap["res_out"] = nc.dram_tensor("res_out", [D, T], F32, kind="ExternalOutput").ap()

    a = alloc_common(nc)
    a["pswap"] = nc.alloc_sbuf_tensor("pswap", [P, P], BF16)
    with nc.Block() as block:
        c = Ctx(nc, block)
        c.gap = gap
        emit_load_consts(c, a)

        @c.block.sync
        def _(sync):
            c.inc(sync.dma_start(out=a["pswap"][:], in_=gap["pswapB"]), "sld", 16)

        emit_load_res(c, a, gap["res_in"])
        emit_attn_a(c, a, gap)
        emit_store_res(c, a, gap["res_out"], [("vev", c.n("vev"))])
    return nc



def emit_attn_b(c, a, aps, cc_in, cc_out):
    b1 = a["b1"][:]; f1 = a["f1"][:]
    xn = a["b2"][:]
    Abuf = b1[:, 0:16384]
    piece = [b1[:, 18432:19456], b1[:, 19456:20480]]
    kbuf = f1[:, 0:2048]
    thbuf = f1[:, 2048:4096]
    abuf = f1[:, 4096:6144]
    gbuf = f1[:, 6144:8192]
    xlbuf = f1[:, 8192:10240]
    bnd = f1[:, 10240:10304]
    Xsel = f1[:, 10304:10336]
    PP = f1[:, 10336:10368]
    Ptmp = f1[:, 10368:10400]
    gath = f1[:, 10400:10912]
    msku = a["msku"]
    pb = a["pb"]
    bq = a["biasq"]

    @c.block.sync
    def _(sync):
        c.inc(sync.dma_start(out=bq[:, 0:DC], in_=aps["bz"]), "sl2", 16)
        c.inc(sync.dma_start(out=bq[:, DC:2 * DC], in_=aps["bh"]), "sl2", 16)
        c.inc(sync.dma_start(out=a["selcol"][:], in_=aps["selcol"]), "sl2", 16)
    sld_b = c.n("sl2")
    ln_end = c.n("vev")

    wsrcs = {0: aps["WzT"], 1: aps["WhT"]}
    ev_marks = {}
    bnd_done = None

    for oc in range(DC):
        for pj in range(2):
            g2 = oc * 2 + pj
            lane = "wsA" if g2 % 2 == 0 else "wsB"
            dst = kbuf if pj == 0 else thbuf
            t_base = c.n("tmm")

            @c.block.gpsimd
            def _(g, g2=g2, lane=lane, pj=pj, oc=oc):
                c.w(g, "tmm", t_base)
                c.inc(g.dma_start(
                    out=piece[g2 % 2].rearrange("p (cc f) -> p cc f", cc=DC),
                    in_=wsrcs[pj].rearrange("(cc p) f -> p cc f", p=P)
                        [:, :, oc * 128:(oc + 1) * 128]), lane, 16)
            wmark = c.n(lane)

            mm_marks = []

            @c.block.tensor
            def _(t, g2=g2, lane=lane, wmark=wmark):
                c.w(t, lane, wmark)
                if g2 == 0:
                    c.w(t, "vev", ln_end)
                for ti in range(4):
                    bank = (g2 * 4 + ti) % 4
                    if bank in ev_marks:
                        c.w(t, "sev", ev_marks[bank])
                    for cc in range(DC):
                        ins = t.matmul(pb[bank][:, 0:512],
                                       piece[g2 % 2][:, cc * 128:(cc + 1) * 128],
                                       xn[:, cc * T + ti * 512: cc * T + (ti + 1) * 512],
                                       start=(cc == 0), stop=(cc == DC - 1))
                    mm_marks.append(c.inc(ins, "tmm", 1))

            vgate = c.n("vsq")

            @c.block.scalar
            def _(s, g2=g2, pj=pj, oc=oc, dst=dst, mm_marks=mm_marks, vgate=vgate):
                c.w(s, "vsq", vgate)
                c.w(s, "sl2", sld_b)
                for ti in range(4):
                    bank = (g2 * 4 + ti) % 4
                    c.w(s, "tmm", mm_marks[ti])
                    m = c.inc(s.activation(dst[:, ti * 512:(ti + 1) * 512],
                                           pb[bank][:, 0:512], AF.Identity,
                                           bias=bq[:, pj * DC + oc: pj * DC + oc + 1]),
                              "sev", 1)
                    ev_marks[bank] = m

        sgate = c.n("sev")

        @c.block.scalar
        def _(s, oc=oc, sgate=sgate):
            c.w(s, "sev", sgate)
            m1 = c.inc(s.activation(abuf[:, 0:T], kbuf[:, 0:T], AF.Sigmoid,
                                    scale=-1.0), "sev", 1)
            m2 = c.inc(s.activation(gbuf[:, 0:T], thbuf[:, 0:T], AF.Sigmoid),
                       "sev", 1)
            c.cnt["_ab_s"] = (m1, m2)

        m1, m2 = None, None

        @c.block.vector
        def _(v, oc=oc):
            m1, m2 = c.cnt["_ab_s"]
            c.w(v, "sev", m2)
            mk = c.inc(v.tensor_single_scalar(msku[:, 0:T], thbuf[:, 0:T],
                                              0.0, ALU.is_ge), "vsq", 1)
            c.w(v, "vsq", mk)
            mk = c.inc(v.tensor_scalar(thbuf[:, 0:T], thbuf[:, 0:T], 0.5,
                                       None, ALU.add), "vsq", 1)
            c.w(v, "vsq", mk)
            mk = c.inc(v.copy_predicated(gbuf[:, 0:T], msku[:, 0:T],
                                         thbuf[:, 0:T]), "vsq", 1)
            c.w(v, "vsq", mk)
            mk = c.inc(v.tensor_scalar(kbuf[:, 0:T], abuf[:, 0:T], -1.0, 1.0,
                                       ALU.mult, ALU.add), "vsq", 1)
            c.w(v, "vsq", mk)
            mk = c.inc(v.tensor_mul(thbuf[:, 0:T], kbuf[:, 0:T], gbuf[:, 0:T]),
                       "vsq", 1)
            c.w(v, "vsq", mk)
            for bseg in range(B):
                sl = slice(bseg * SL, (bseg + 1) * SL)
                mk = c.inc(v.tensor_tensor_scan(xlbuf[:, sl], abuf[:, sl],
                                                thbuf[:, sl], 0.0,
                                                ALU.mult, ALU.add), "vsq", 1)
                c.w(v, "vsq", mk)
                mk = c.inc(v.tensor_tensor_scan(
                    Abuf[:, oc * T + bseg * SL: oc * T + (bseg + 1) * SL],
                    abuf[:, sl], abuf[:, sl], 1.0,
                    ALU.mult, ALU.bypass), "vsq", 1)
                c.w(v, "vsq", mk)
            rsl = a["res"][:, oc * T:(oc + 1) * T]
            mk = c.inc(v.tensor_add(rsl, rsl, xlbuf[:, 0:T]), "vsq", 1)
            c.w(v, "vsq", mk)
            ins = None
            for bseg in range(B):
                v.tensor_copy(bnd[:, oc * 4 + bseg: oc * 4 + bseg + 1],
                              xlbuf[:, bseg * SL + SL - 1: bseg * SL + SL])
                ins = v.tensor_copy(
                    bnd[:, 32 + oc * 4 + bseg: 32 + oc * 4 + bseg + 1],
                    Abuf[:, oc * T + bseg * SL + SL - 1: oc * T + bseg * SL + SL])
            c.inc(ins, "vev", 1)
    bnd_done = c.n("vev")

    @c.block.gpsimd
    def _(g):
        c.w(g, "vev", bnd_done)
        c.inc(g.dma_start(out=cc_in, in_=bnd), "ccd", 16)
        c.w(g, "ccd", c.n("ccd"))
        c.inc(g.collective_compute(
            "AllGather", mybir.AluOpType.bypass,
            replica_groups=[list(range(NCORES))],
            ins=[cc_in.opt()], outs=[cc_out.opt()]), "ccs", 1)
        c.w(g, "ccs", c.n("ccs"))
        c.inc(g.dma_start(out=gath.rearrange("p (r e) -> p r e", r=NCORES),
                          in_=cc_out.rearrange("(r p) e -> p r e", p=P)),
              "ccd", 16)
    cc_done = c.n("ccd")

    @c.block.vector
    def _(v):
        c.w(v, "ccd", cc_done)
        c.w(v, "sl2", sld_b)
        mk = c.inc(v.memset(Xsel[:, 0:32], 0.0), "vsq", 1)
        c.w(v, "vsq", mk)
        mk = c.inc(v.memset(PP[:, 0:32], 0.0), "vsq", 1)
        c.w(v, "vsq", mk)
        for r in range(NCORES):
            xl_r = gath[:, r * 64: r * 64 + 32]
            A_r = gath[:, r * 64 + 32: r * 64 + 64]
            mk = c.inc(v.scalar_tensor_tensor(Xsel[:, 0:32], PP[:, 0:32],
                                              a["selcol"][:, r:r + 1],
                                              Xsel[:, 0:32],
                                              ALU.mult, ALU.add), "vsq", 1)
            c.w(v, "vsq", mk)
            if r < NCORES - 1:
                mk = c.inc(v.tensor_mul(Ptmp[:, 0:32], A_r, PP[:, 0:32]),
                           "vsq", 1)
                c.w(v, "vsq", mk)
                mk = c.inc(v.tensor_add(PP[:, 0:32], xl_r, Ptmp[:, 0:32]),
                           "vsq", 1)
                c.w(v, "vsq", mk)
        ins = None
        for oc in range(DC):
            for bseg in range(B):
                rsl = a["res"][:, oc * T + bseg * SL: oc * T + (bseg + 1) * SL]
                mk = c.inc(v.scalar_tensor_tensor(
                    rsl, Abuf[:, oc * T + bseg * SL: oc * T + (bseg + 1) * SL],
                    Xsel[:, oc * 4 + bseg: oc * 4 + bseg + 1], rsl,
                    ALU.mult, ALU.add), "vsq", 1)
                c.w(v, "vsq", mk)
        c.inc(v.engine_nop(), "vev", 1)


def build_attn_b_launch():
    nc = bass.Bass("TRN2", target_bir_lowering=False)
    gap = {}
    gap["res_in"] = nc.dram_tensor("res_in", [D, T], F32, kind="ExternalInput").ap()
    gap["identB"] = nc.dram_tensor("identB", [P, P], BF16, kind="ExternalInput").ap()
    gap["ones_colB"] = nc.dram_tensor("ones_colB", [P, 1], BF16, kind="ExternalInput").ap()
    gap["ones_rowB"] = nc.dram_tensor("ones_rowB", [1, P], BF16, kind="ExternalInput").ap()
    gap["WzT"] = nc.dram_tensor("WzT", [D, D], F32, kind="ExternalInput").ap()
    gap["WhT"] = nc.dram_tensor("WhT", [D, D], F32, kind="ExternalInput").ap()
    gap["bz"] = nc.dram_tensor("bz", [P, DC], F32, kind="ExternalInput").ap()
    gap["bh"] = nc.dram_tensor("bh", [P, DC], F32, kind="ExternalInput").ap()
    gap["selcol"] = nc.dram_tensor("selcol", [P, NCORES], F32, kind="ExternalInput").ap()
    gap["res_out"] = nc.dram_tensor("res_out", [D, T], F32, kind="ExternalOutput").ap()
    cc_in = nc.dram_tensor("ccb_in", [P, 64], F32).ap()
    cc_out = nc.dram_tensor("ccb_out", [P * NCORES, 64], F32).ap()

    a = alloc_common(nc)
    a["msku"] = a["b3"][:, 0:1024].bitcast(U8)
    a["selcol"] = nc.alloc_sbuf_tensor("s_selcol", [P, NCORES], F32)
    with nc.Block() as block:
        c = Ctx(nc, block)
        c.gap = gap
        emit_load_consts(c, a)
        emit_load_res(c, a, gap["res_in"])
        emit_ln(c, a, a["b2"][:], pre=[("sld", c.n("sld"))])
        emit_attn_b(c, a, gap, cc_in, cc_out)
        emit_store_res(c, a, gap["res_out"], [("vev", c.n("vev"))])
    return nc



def emit_l1ffn(c, a, aps, cc_in, cc_out):
    b1 = a["b1"][:]; b2 = a["b2"][:]; f1 = a["f1"][:]; b3 = a["b3"][:]
    xn = b2
    xr = b1[:, 0:16384]
    xkb = f1[:, 0:8192].bitcast(BF16)
    piece = [b1[:, 18432:19456], b1[:, 19456:20480]]
    bcol = b3[:, 1024:1056]
    gath = b3[:, 1056:1312]
    savedxn = b3[:, 1312:1344]
    tmpA = b3[:, 3104:3616]
    tmpR2 = b3[:, 2048:3072]
    prevc = f1[:, 8192:8224]
    prsc_r = f1[:, 8224:8256]
    prsc_k = f1[:, 8256:8288]
    ytmp = f1[:, 8704:9216]
    pb = a["pb"]
    bq = a["biasq"]
    ln_end = c.n("vev")

    @c.block.sync
    def _(sync):
        c.inc(sync.dma_start(out=bq[:, 0:DC], in_=aps["F1mur"]), "sl2", 16)
        c.inc(sync.dma_start(out=bq[:, DC:2 * DC], in_=aps["F1muk"]), "sl2", 16)
        c.inc(sync.dma_start(out=a["selcol"][:], in_=aps["selprev"]), "sl2", 16)
    sl2_b = c.n("sl2")

    @c.block.vector
    def _(v):
        c.w(v, "sl2", sl2_b)
        c.w(v, "vev", ln_end)
        mk = c.inc(v.tensor_scalar(bq[:, 16:32], bq[:, 0:16], -1.0, 1.0,
                                   ALU.mult, ALU.add), "vsq", 1)
        c.w(v, "vsq", mk)
        ins = None
        for ch in range(DC):
            for bseg in range(B):
                col = ch * T + bseg * SL
                v.tensor_copy(bcol[:, ch * 4 + bseg: ch * 4 + bseg + 1],
                              xn[:, col + SL - 1: col + SL])
                ins = v.tensor_copy(savedxn[:, ch * 4 + bseg: ch * 4 + bseg + 1],
                                    xn[:, col: col + 1])
        c.inc(ins, "vev", 1)
    bcol_done = c.n("vev")

    @c.block.gpsimd
    def _(g):
        c.w(g, "vev", bcol_done)
        c.inc(g.dma_start(out=cc_in, in_=bcol), "ccd", 16)
        c.w(g, "ccd", c.n("ccd"))
        c.inc(g.collective_compute(
            "AllGather", mybir.AluOpType.bypass,
            replica_groups=[list(range(NCORES))],
            ins=[cc_in.opt()], outs=[cc_out.opt()]), "ccs", 1)
        c.w(g, "ccs", c.n("ccs"))
        c.inc(g.dma_start(out=gath.rearrange("p (r e) -> p r e", r=NCORES),
                          in_=cc_out.rearrange("(r p) e -> p r e", p=P)),
              "ccd", 16)
    cc_done = c.n("ccd")

    @c.block.vector
    def _(v):
        for dst, mu0, om0 in ((xr, 0, 16), (xkb, 8, 24)):
            for ch in range(DC):
                for bseg in range(B):
                    col = ch * T + bseg * SL
                    mk = c.inc(v.tensor_scalar(tmpA[:, 0:SL], xn[:, col:col + SL],
                                               bq[:, mu0 + ch:mu0 + ch + 1],
                                               None, ALU.mult), "vsq", 1)
                    c.w(v, "vsq", mk)
                    mk = c.inc(v.scalar_tensor_tensor(
                        dst[:, col + 1: col + SL], xn[:, col: col + SL - 1],
                        bq[:, om0 + ch:om0 + ch + 1], tmpA[:, 1:SL],
                        ALU.mult, ALU.add), "vsq", 1)
                    c.w(v, "vsq", mk)
        c.inc(v.engine_nop(), "vev", 1)

    @c.block.vector
    def _(v):
        c.w(v, "ccd", cc_done)
        mk = c.inc(v.memset(prevc[:, 0:32], 0.0), "vsq", 1)
        c.w(v, "vsq", mk)
        for r in range(NCORES):
            mk = c.inc(v.scalar_tensor_tensor(
                prevc[:, 0:32], gath[:, r * 32:(r + 1) * 32],
                a["selcol"][:, r:r + 1], prevc[:, 0:32],
                ALU.mult, ALU.add), "vsq", 1)
            c.w(v, "vsq", mk)
        for ch in range(DC):
            mk = c.inc(v.tensor_scalar(prsc_r[:, ch * 4:(ch + 1) * 4],
                                       prevc[:, ch * 4:(ch + 1) * 4],
                                       bq[:, 16 + ch:17 + ch], None, ALU.mult),
                       "vsq", 1)
            c.w(v, "vsq", mk)
            mk = c.inc(v.tensor_scalar(prsc_k[:, ch * 4:(ch + 1) * 4],
                                       prevc[:, ch * 4:(ch + 1) * 4],
                                       bq[:, 24 + ch:25 + ch], None, ALU.mult),
                       "vsq", 1)
            c.w(v, "vsq", mk)
        ins = None
        for dst, mu0, prs in ((xr, 0, prsc_r), (xkb, 8, prsc_k)):
            for ch in range(DC):
                for bseg in range(B):
                    col = ch * T + bseg * SL
                    j = ch * 4 + bseg
                    ins = v.scalar_tensor_tensor(
                        dst[:, col: col + 1], savedxn[:, j:j + 1],
                        bq[:, mu0 + ch:mu0 + ch + 1], prs[:, j:j + 1],
                        ALU.mult, ALU.add)
                    mk = c.inc(ins, "vsq", 1)
                    c.w(v, "vsq", mk)
        c.inc(v.engine_nop(), "vev", 1)
    mix_done = c.n("vev")

    ev_marks = {}

    def proj(w_ap, rhs, kind):
        for oc in range(DC):
            g2 = oc % 2
            lane = "wsA" if g2 == 0 else "wsB"
            t_base = c.n("tmm")

            @c.block.gpsimd
            def _(g, oc=oc, lane=lane, g2=g2, t_base=t_base, w_ap=w_ap):
                c.w(g, "tmm", t_base)
                c.inc(g.dma_start(
                    out=piece[g2].rearrange("p (cc f) -> p cc f", cc=DC),
                    in_=w_ap.rearrange("(cc p) f -> p cc f", p=P)
                        [:, :, oc * 128:(oc + 1) * 128]), lane, 16)
            wmark = c.n(lane)

            mm_marks = []

            @c.block.tensor
            def _(t, oc=oc, lane=lane, g2=g2, wmark=wmark, rhs=rhs,
                  mm_marks=mm_marks):
                c.w(t, lane, wmark)
                if oc == 0:
                    c.w(t, "vev", mix_done)
                for ti in range(4):
                    bank = ti % 4
                    if bank in ev_marks:
                        c.w(t, ev_marks[bank][0], ev_marks[bank][1])
                    for cc in range(DC):
                        ins = t.matmul(pb[bank][:, 0:512],
                                       piece[g2][:, cc * 128:(cc + 1) * 128],
                                       rhs[:, cc * T + ti * 512: cc * T + (ti + 1) * 512],
                                       start=(cc == 0), stop=(cc == DC - 1))
                    mm_marks.append(c.inc(ins, "tmm", 1))

            if kind == "r":
                @c.block.scalar
                def _(s, oc=oc, mm_marks=mm_marks):
                    for ti in range(4):
                        c.w(s, "tmm", mm_marks[ti])
                        m = c.inc(s.activation(
                            b2[:, oc * T + ti * 512: oc * T + (ti + 1) * 512],
                            pb[ti % 4][:, 0:512], AF.Sigmoid), "sev", 1)
                        ev_marks[ti % 4] = ("sev", m)
            elif kind == "k":
                @c.block.scalar
                def _(s, oc=oc, mm_marks=mm_marks):
                    for ti in range(4):
                        gi = oc * 4 + ti
                        if gi >= 2:
                            c.w(s, "vev", c.cnt["_f1kv"] + gi - 1)
                        c.w(s, "tmm", mm_marks[ti])
                        c.inc(s.activation(tmpR2[:, (ti % 2) * 512:(ti % 2) * 512 + 512],
                                           pb[ti % 4][:, 0:512],
                                           AF.Relu), "sev", 1)

                @c.block.vector
                def _(v, oc=oc, mm_marks=mm_marks):
                    for ti in range(4):
                        c.w(v, "sev", c.cnt["_f1k"] + oc * 4 + ti + 1)
                        m = c.inc(v.tensor_mul(
                            xr[:, oc * T + ti * 512: oc * T + (ti + 1) * 512],
                            pb[ti % 4][:, 0:512],
                            tmpR2[:, (ti % 2) * 512:(ti % 2) * 512 + 512]),
                            "vev", 1)
                        ev_marks[ti % 4] = ("vev", m)
            else:
                @c.block.vector
                def _(v, oc=oc, mm_marks=mm_marks):
                    for ti in range(4):
                        c.w(v, "tmm", mm_marks[ti])
                        if c.cnt.get("_f1y") is not None:
                            c.w(v, "vev", c.cnt["_f1y"])
                        sl = slice(oc * T + ti * 512, oc * T + (ti + 1) * 512)
                        mk = c.inc(v.tensor_mul(ytmp[:, 0:512],
                                                pb[ti % 4][:, 0:512], b2[:, sl]),
                                   "vsq", 1)
                        c.w(v, "vsq", mk)
                        m = c.inc(v.tensor_add(a["res"][:, sl], a["res"][:, sl],
                                               ytmp[:, 0:512]), "vev", 1)
                        c.cnt["_f1y"] = m
                        ev_marks[ti % 4] = ("vev", m)

    c.cnt["_f1y"] = None
    proj(aps["F1WrT"], xr, "r")
    c.cnt["_f1k"] = c.n("sev")
    c.cnt["_f1kv"] = c.n("vev")
    kgate = c.n("tmm")

    @c.block.vector
    def _(v):
        c.w(v, "tmm", kgate)
    proj(aps["F1WkT"], xkb, "k")
    proj(aps["F1WvT"], xr, "y")


def build_l1ffn_launch():
    nc = bass.Bass("TRN2", target_bir_lowering=False)
    gap = {}
    gap["res_in"] = nc.dram_tensor("res_in", [D, T], F32, kind="ExternalInput").ap()
    gap["identB"] = nc.dram_tensor("identB", [P, P], BF16, kind="ExternalInput").ap()
    gap["ones_colB"] = nc.dram_tensor("ones_colB", [P, 1], BF16, kind="ExternalInput").ap()
    gap["ones_rowB"] = nc.dram_tensor("ones_rowB", [1, P], BF16, kind="ExternalInput").ap()
    gap["F1WrT"] = nc.dram_tensor("F1WrT", [D, D], F32, kind="ExternalInput").ap()
    gap["F1WkT"] = nc.dram_tensor("F1WkT", [D, D], F32, kind="ExternalInput").ap()
    gap["F1WvT"] = nc.dram_tensor("F1WvT", [D, D], F32, kind="ExternalInput").ap()
    gap["F1mur"] = nc.dram_tensor("F1mur", [P, DC], F32, kind="ExternalInput").ap()
    gap["F1muk"] = nc.dram_tensor("F1muk", [P, DC], F32, kind="ExternalInput").ap()
    gap["selprev"] = nc.dram_tensor("selprev", [P, NCORES], F32, kind="ExternalInput").ap()
    gap["res_out"] = nc.dram_tensor("res_out", [D, T], F32, kind="ExternalOutput").ap()
    cc_in = nc.dram_tensor("ccf_in", [P, 32], BF16).ap()
    cc_out = nc.dram_tensor("ccf_out", [P * NCORES, 32], BF16).ap()

    a = alloc_common(nc)
    a["selcol"] = nc.alloc_sbuf_tensor("s_selcol", [P, NCORES], F32)
    with nc.Block() as block:
        c = Ctx(nc, block)
        c.gap = gap
        emit_load_consts(c, a)
        emit_load_res(c, a, gap["res_in"])
        emit_ln(c, a, a["b2"][:], pre=[("sld", c.n("sld"))])
        emit_l1ffn(c, a, gap, cc_in, cc_out)
        emit_store_res(c, a, gap["res_out"], [("vev", c.n("vev"))])
    return nc




def emit_l1attn_a(c, a, aps, cc_in, cc_out):
    b1 = a["b1"][:]; b2 = a["b2"][:]; f1 = a["f1"][:]; b3 = a["b3"][:]
    xn = b2
    qf = b1[:, 0:16384]
    kf = f1[:, 0:8192].bitcast(BF16)
    piece = [f1[:, 8192:8704].bitcast(BF16), f1[:, 8704:9216].bitcast(BF16)]
    scr = [f1[:, 9216:9728], f1[:, 9728:10240]]
    wv_sb = b1[:, 0:8192]
    pb = a["pb"]; pt = a["pt"]
    bq = a["biasq"]
    KVall = f1[:, 9216:10256].bitcast(BF16)
    ktms = [piece[0][:, 0:64], piece[0][:, 64:128]]
    stg = [f1[:, 10256:10512], f1[:, 10512:10768]]
    k1g = f1[:, 10768:10896].bitcast(BF16)
    Xf = f1[:, 10896:10960]
    K1X = f1[:, 10960:10961]
    stf = f1[:, 10968:11033]
    KVbf = f1[:, 11040:11073].bitcast(BF16)
    mskT = f1[:, 11073:11137].bitcast(BF16)
    rdrow = f1[0:1, 11137:11265] if False else f1[0:1, 11137:11201]
    rdrow = f1[0:1, 11200:11264]
    rdrow = f1[0:1, 11137:11264]
    pbB = a["pb"]

    def vtm(tt):
        if tt < 8:
            return b3[:, tt * 1024:(tt + 1) * 1024]
        return b1[:, 16384 + (tt - 8) * 1024: 16384 + (tt - 7) * 1024]

    ln_end = c.n("vev")

    @c.block.sync
    def _(sync):
        c.inc(sync.dma_start(out=bq[:, 0:DC], in_=aps["A1bq"]), "sl2", 16)
        c.inc(sync.dma_start(out=bq[:, DC:16], in_=aps["A1bk"]), "sl2", 16)
        c.inc(sync.dma_start(out=bq[:, 16:24], in_=aps["A1bv"]), "sl2", 16)
        c.inc(sync.dma_start(out=bq[:, 24:32], in_=aps["A1bo"]), "sl2", 16)
        c.inc(sync.dma_start(out=a["selcol"][:], in_=aps["selcum"]), "sl2", 16)
    sl2_b = c.n("sl2")

    @c.block.gpsimd
    def _(g):
        c.w(g, "vev", ln_end)
        c.inc(g.dma_start(out=wv_sb.rearrange("p (cc f) -> p cc f", cc=DC),
                          in_=aps["A1WvT"].rearrange("(cc p) f -> p cc f", p=P)),
              "gld", 16)
    gv = c.n("gld")
    t_v = c.n("tmm")
    v_v = c.n("vev")

    @c.block.tensor
    def _(t):
        c.w(t, "gld", gv)
        c.w(t, "vev", ln_end)
        for gi in range(32):
            tt, half = gi // 2, gi % 2
            if gi >= 4:
                c.w(t, "vev", v_v + gi - 3)
            for cc in range(DC):
                ins = t.matmul(pb[gi % 4][:, 0:512],
                               xn[:, cc * T + tt * P: cc * T + (tt + 1) * P],
                               wv_sb[:, cc * 1024 + half * 512:
                                     cc * 1024 + (half + 1) * 512],
                               start=(cc == 0), stop=(cc == DC - 1))
            c.inc(ins, "tmm", 1)

    @c.block.vector
    def _(v):
        for gi in range(32):
            tt, half = gi // 2, gi % 2
            c.w(v, "tmm", t_v + gi + 1)
            c.inc(v.tensor_copy(vtm(tt)[:, half * 512:(half + 1) * 512],
                                pb[gi % 4][:, 0:512]), "vev", 1)
    vtm_done = c.n("vev")

    qk_ev = {}

    def qkproj(w_ap, dst, b0, first):
        ev_marks = qk_ev
        vq0 = c.n("vev")
        for oc in range(DC):
            g2 = oc % 2
            lane = "wsA" if g2 == 0 else "wsB"
            t_base = c.n("tmm")

            @c.block.gpsimd
            def _(g, oc=oc, lane=lane, g2=g2, t_base=t_base, w_ap=w_ap):
                c.w(g, "tmm", t_base)
                c.inc(g.dma_start(
                    out=piece[g2].rearrange("p (cc f) -> p cc f", cc=DC),
                    in_=w_ap.rearrange("(cc p) f -> p cc f", p=P)
                        [:, :, oc * 128:(oc + 1) * 128]), lane, 16)
            wmark = c.n(lane)

            mm_marks = []

            @c.block.tensor
            def _(t, oc=oc, lane=lane, g2=g2, wmark=wmark, first=first,
                  mm_marks=mm_marks):
                c.w(t, lane, wmark)
                if oc == 0 and first:
                    c.w(t, "vev", vtm_done)
                for ti in range(4):
                    bank = ti % 4
                    if bank in ev_marks:
                        c.w(t, "vev", ev_marks[bank])
                    for cc in range(DC):
                        ins = t.matmul(pb[bank][:, 0:512],
                                       piece[g2][:, cc * 128:(cc + 1) * 128],
                                       xn[:, cc * T + ti * 512: cc * T + (ti + 1) * 512],
                                       start=(cc == 0), stop=(cc == DC - 1))
                    mm_marks.append(c.inc(ins, "tmm", 1))

            smarks = []

            @c.block.scalar
            def _(s, oc=oc, b0=b0, mm_marks=mm_marks, smarks=smarks, vq0=vq0):
                c.w(s, "sl2", sl2_b)
                for ti in range(4):
                    if oc * 4 + ti >= 2:
                        c.w(s, "vev", vq0 + oc * 4 + ti - 1)
                    else:
                        gd = c.cnt.get("_scr" + str(ti % 2))
                        if gd:
                            c.w(s, "vev", gd)
                    c.w(s, "tmm", mm_marks[ti])
                    sl = slice(oc * T + ti * 512, oc * T + (ti + 1) * 512)
                    bank = ti % 4
                    bcol = bq[:, b0 + oc:b0 + oc + 1]
                    s.activation(dst[:, sl], pb[bank][:, 0:512], AF.Relu,
                                 bias=bcol)
                    mk = c.inc(s.activation(scr[ti % 2][:, 0:512],
                                            pb[bank][:, 0:512], AF.Abs,
                                            bias=bcol), "ssq", 1)
                    c.w(s, "ssq", mk)
                    mk = c.inc(s.activation(scr[ti % 2][:, 0:512],
                                            scr[ti % 2][:, 0:512], AF.Exp,
                                            scale=-1.0), "ssq", 1)
                    c.w(s, "ssq", mk)
                    smarks.append(c.inc(s.activation(scr[ti % 2][:, 0:512],
                                                     scr[ti % 2][:, 0:512],
                                                     AF.Ln,
                                                     bias=a["onesf"][:, 0:1]),
                                        "sev", 1))

            @c.block.vector
            def _(v, oc=oc, mm_marks=mm_marks, smarks=smarks):
                for ti in range(4):
                    c.w(v, "sev", smarks[ti])
                    sl = slice(oc * T + ti * 512, oc * T + (ti + 1) * 512)
                    m = c.inc(v.tensor_add(dst[:, sl], dst[:, sl],
                                           scr[ti % 2][:, 0:512]), "vev", 1)
                    ev_marks[ti % 4] = m
                    c.cnt["_scr" + str(ti % 2)] = m

    qkproj(aps["A1WqT"], qf, 0, True)
    qkproj(aps["A1WkT"], kf, DC, False)
    proj_v = c.n("vev")

    tA = c.n("tmm")
    rA = c.n("vrt")
    vA = c.n("vev")

    @c.block.tensor
    def _(t):
        c.w(t, "vev", proj_v)
        for z in range(64):
            bseg, h = z // NH, z % NH
            qb = 64 * (h % 2)
            qc = h // 2
            for i in range(4):
                gi = z * 4 + i
                if gi >= 2:
                    c.w(t, "vrt", rA + gi - 1)
                ins = t.transpose(pt[gi % 2][0:128, 0:64],
                                  kf[qb:qb + 64,
                                     qc * T + bseg * SL + i * 128:
                                     qc * T + bseg * SL + (i + 1) * 128],
                                  a["ident"][qb:qb + 64, qb:qb + 64])
                c.inc(ins, "tmm", 1)
                c.w(t, "vrt", rA + gi + 1)
                if i == 0 and z > 0:
                    c.w(t, "vev", vA + z)
                tp = (0, qb)
                vt = vtm(bseg * 4 + i)[:, h * 64:(h + 1) * 64]
                t.matmul(pb[4][qb:qb + 64, 0:64], ktms[gi % 2][:, 0:64], vt,
                         start=(i == 0), stop=(i == 3), tile_position=tp)
                ins = t.matmul(pb[5][qb:qb + 64, 0:1], ktms[gi % 2][:, 0:64],
                               a["ones_col"][:], start=(i == 0), stop=(i == 3),
                               tile_position=tp)
                c.inc(ins, "tmm", 1)

    @c.block.vector
    def _(v):
        for z in range(64):
            qb = 64 * (z % 2)
            zp = z // 2
            for i in range(4):
                gi = z * 4 + i
                c.w(v, "tmm", tA + 2 * gi + 1)
                c.inc(v.tensor_copy(ktms[gi % 2][:, 0:64],
                                    pt[gi % 2][0:128, 0:64]), "vrt", 1)
            c.w(v, "tmm", tA + 2 * (z * 4 + 4))
            mk = c.inc(v.tensor_copy(KVall[qb:qb + 64, zp * 64:(zp + 1) * 64],
                                     pb[4][qb:qb + 64, 0:64]), "vsq", 1)
            c.w(v, "vsq", mk)
            c.inc(v.tensor_copy(KVall[qb:qb + 64, 2048 + zp:2048 + zp + 1],
                                pb[5][qb:qb + 64, 0:1]), "vev", 1)
    passa_v = c.n("vev")

    @c.block.gpsimd
    def _(g):
        c.w(g, "vev", passa_v)
        c.inc(g.dma_start(out=cc_in, in_=KVall[:, 0:2080]), "ccd", 16)
        c.w(g, "ccd", c.n("ccd"))
        c.inc(g.collective_compute(
            "AllGather", mybir.AluOpType.bypass,
            replica_groups=[list(range(NCORES))],
            ins=[cc_in.opt()], outs=[cc_out.opt()]), "ccs", 1)
        c.w(g, "ccs", c.n("ccs"))
        c.inc(g.dma_start(
            out=k1g[:, 0:256].rearrange("p (r e) -> p r e", r=NCORES),
            in_=cc_out.rearrange("(r p) e -> p r e", p=P)[:, :, 2048:2080]),
            "ccd", 16)
    ccK = c.n("ccd")

    tB = c.n("tmm")
    vB = c.n("vev")
    sl3_0 = c.n("sl3")
    sl4_0 = c.n("sl4")
    TI = 7
    VI = 11
    VZ = 20 + 4 * VI

    def tmmB(z, i, k):
        return tB + z * 4 * TI + i * TI + k

    def vevB(z, i, k):
        return vB + z * VZ + 20 + i * VI + k

    def vevBpre(z, k):
        return vB + z * VZ + k

    @c.block.sync
    def _(sync):
        for z in range(64):
            qb = 64 * (z % 2)
            zp = z // 2
            if z >= 2:
                c.w(sync, "vev", vevBpre(z - 2, 20))
            else:
                c.w(sync, "ccd", ccK)
            c.inc(sync.dma_start(
                out=stg[z % 2].bitcast(BF16)[qb:qb + 64, 0:512].rearrange(
                    "p (r e) -> p r e", r=NCORES),
                in_=cc_out.rearrange("(r p) e -> p r e", p=P)
                    [qb:qb + 64, :, zp * 64:(zp + 1) * 64]),
                "sl3" if z % 2 == 0 else "sl4", 16)

    @c.block.vector
    def _(v):
        c.w(v, "ccd", ccK)
        c.w(v, "sl2", sl2_b)
        for z in range(64):
            bseg, h = z // NH, z % NH
            qb = 64 * (h % 2)
            zp = z // 2
            if z % 2 == 0:
                c.w(v, "sl3", sl3_0 + (z // 2 + 1) * 16)
            else:
                c.w(v, "sl4", sl4_0 + (z // 2 + 1) * 16)
            sg = stg[z % 2].bitcast(BF16)
            mk = c.inc(v.memset(Xf[qb:qb + 64, 0:64], 0.0), "vev", 1)
            c.w(v, "vev", mk)
            mk = c.inc(v.memset(K1X[qb:qb + 64, 0:1], 0.0), "vev", 1)
            c.w(v, "vev", mk)
            for r in range(NCORES):
                mk = c.inc(v.scalar_tensor_tensor(
                    Xf[qb:qb + 64, 0:64], sg[qb:qb + 64, r * 64:(r + 1) * 64],
                    a["selcol"][qb:qb + 64, r:r + 1], Xf[qb:qb + 64, 0:64],
                    ALU.mult, ALU.add), "vev", 1)
                c.w(v, "vev", mk)
                mk = c.inc(v.scalar_tensor_tensor(
                    K1X[qb:qb + 64, 0:1],
                    k1g[qb:qb + 64, r * 32 + zp: r * 32 + zp + 1],
                    a["selcol"][qb:qb + 64, r:r + 1], K1X[qb:qb + 64, 0:1],
                    ALU.mult, ALU.add), "vev", 1)
                c.w(v, "vev", mk)
            mk = c.inc(v.tensor_copy(stf[qb:qb + 64, 0:64], Xf[qb:qb + 64, 0:64]),
                       "vev", 1)
            c.w(v, "vev", mk)
            mk = c.inc(v.tensor_copy(stf[qb:qb + 64, 64:65],
                                     K1X[qb:qb + 64, 0:1]), "vsq", 1)
            c.w(v, "vsq", mk)
            v.tensor_copy(KVbf[qb:qb + 64, 0:64], stf[qb:qb + 64, 0:64])
            mk = c.inc(v.tensor_copy(KVbf[qb:qb + 64, 64:65],
                                     stf[qb:qb + 64, 64:65]), "vev", 1)
            c.w(v, "vev", mk)
            assert c.n("vev") == vevBpre(z, 20), (z, c.n("vev"), vevBpre(z, 20))
            for i in range(4):
                c.w(v, "tmm", tmmB(z, i, 1))
                mk = c.inc(v.tensor_copy(ktms[(z * 4 + i) % 2][:, 0:64],
                                         pt[(z * 4 + i) % 2][0:128, 0:64]),
                           "vev", 1)
                c.w(v, "tmm", tmmB(z, i, 2))
                mk = c.inc(v.tensor_mul(mskT[:, 0:128], pb[0][:, 0:128],
                                        a["cmask"][:]), "vev", 1)
                c.w(v, "vev", mk)
                c.w(v, "tmm", tmmB(z, i, 4))
                mk = c.inc(v.reciprocal(rdrow[:, 0:127] if False else
                                        scr[0][0:1, 0:128], pb[2][0:1, 0:128]),
                           "vev", 1)
                c.w(v, "vev", mk)
                mk = c.inc(v.tensor_copy(scr[0][0:1, 128:192].bitcast(BF16)[0:1, 0:128],
                                         scr[0][0:1, 0:128]), "vev", 1)
                c.w(v, "vev", mk)
                c.w(v, "tmm", tmmB(z, i, 6))
                mk = c.inc(v.tensor_add(stf[qb:qb + 64, 0:64],
                                        stf[qb:qb + 64, 0:64],
                                        pb[3][qb:qb + 64, 0:64]), "vev", 1)
                c.w(v, "vev", mk)
                mk = c.inc(v.tensor_add(stf[qb:qb + 64, 64:65],
                                        stf[qb:qb + 64, 64:65],
                                        pb[4][qb:qb + 64, 0:1]), "vev", 1)
                c.w(v, "vev", mk)
                mk = c.inc(v.tensor_copy(KVbf[qb:qb + 64, 0:64],
                                         stf[qb:qb + 64, 0:64]), "vev", 1)
                c.w(v, "vev", mk)
                mk = c.inc(v.tensor_copy(KVbf[qb:qb + 64, 64:65],
                                         stf[qb:qb + 64, 64:65]), "vev", 1)
                c.w(v, "vev", mk)
                c.w(v, "tmm", tmmB(z, i, 7))
                denB = scr[1][:, 0:64].bitcast(BF16)[qb:qb + 64, 0:128]
                mk = c.inc(v.tensor_copy(denB, pb[5][qb:qb + 64, 0:128]),
                           "vev", 1)
                c.w(v, "vev", mk)
                outst = scr[1][:, 64:128].bitcast(BF16)[qb:qb + 64, 0:128]
                mk = c.inc(v.tensor_mul(outst, pb[1][qb:qb + 64, 0:128], denB),
                           "vev", 1)
                c.w(v, "vev", mk)
                oslice = xn[qb:qb + 64,
                            (h // 2) * T + bseg * SL + i * 128:
                            (h // 2) * T + bseg * SL + (i + 1) * 128]
                mk = c.inc(v.tensor_scalar(oslice, outst,
                                           bq[qb:qb + 64, 16 + h // 2:17 + h // 2],
                                           None, ALU.add), "vev", 1)
                assert c.n("vev") == vevB(z, i, VI), (z, i)

    @c.block.tensor
    def _(t):
        for z in range(64):
            bseg, h = z // NH, z % NH
            qb = 64 * (h % 2)
            qc = h // 2
            col0 = qc * T + bseg * SL
            for i in range(4):
                gi = z * 4 + i
                if gi >= 2:
                    c.w(t, "vev", vevB((gi - 2) // 4, (gi - 2) % 4, 1))
                ins = t.transpose(pt[gi % 2][0:128, 0:64],
                                  kf[qb:qb + 64, col0 + i * 128: col0 + (i + 1) * 128],
                                  a["ident"][qb:qb + 64, qb:qb + 64])
                c.inc(ins, "tmm", 1)
                if gi >= 1:
                    c.w(t, "vev", vevB((gi - 1) // 4, (gi - 1) % 4, 2))
                ins = t.matmul(pb[0][:, 0:128],
                               kf[qb:qb + 64, col0 + i * 128: col0 + (i + 1) * 128],
                               qf[qb:qb + 64, col0 + i * 128: col0 + (i + 1) * 128],
                               start=True, stop=True)
                c.inc(ins, "tmm", 1)
                c.w(t, "vev", vevB(z, i, 2))
                if gi >= 1:
                    c.w(t, "vev", vevB((gi - 1) // 4, (gi - 1) % 4, 10))
                t.matmul(pb[1][qb:qb + 64, 0:128],
                         vtm(bseg * 4 + i)[:, h * 64:(h + 1) * 64],
                         mskT[:, 0:128], start=True, stop=False,
                         tile_position=(0, qb))
                ins = t.matmul(pb[1][qb:qb + 64, 0:128], KVbf[qb:qb + 64, 0:64],
                               qf[qb:qb + 64, col0 + i * 128: col0 + (i + 1) * 128],
                               start=False, stop=True, tile_position=(qb, qb))
                c.inc(ins, "tmm", 1)
                if gi >= 1:
                    c.w(t, "vev", vevB((gi - 1) // 4, (gi - 1) % 4, 3))
                t.matmul(pb[2][0:1, 0:128], a["ones_col"][:], mskT[:, 0:128],
                         start=True, stop=False)
                ins = t.matmul(pb[2][0:1, 0:128], KVbf[qb:qb + 64, 64:65],
                               qf[qb:qb + 64, col0 + i * 128: col0 + (i + 1) * 128],
                               start=False, stop=True)
                c.inc(ins, "tmm", 1)
                c.w(t, "vev", vevB(z, i, 1))
                if gi >= 1:
                    c.w(t, "vev", vevB((gi - 1) // 4, (gi - 1) % 4, 6))
                tp = (0, qb)
                ins = t.matmul(pb[3][qb:qb + 64, 0:64], ktms[gi % 2][:, 0:64],
                               vtm(bseg * 4 + i)[:, h * 64:(h + 1) * 64],
                               start=True, stop=True, tile_position=tp)
                c.inc(ins, "tmm", 1)
                ins = t.matmul(pb[4][qb:qb + 64, 0:1], ktms[gi % 2][:, 0:64],
                               a["ones_col"][:], start=True, stop=True,
                               tile_position=tp)
                c.inc(ins, "tmm", 1)
                c.w(t, "vev", vevB(z, i, 4))
                if gi >= 1:
                    c.w(t, "vev", vevB((gi - 1) // 4, (gi - 1) % 4, 9))
                ins = t.matmul(pb[5][qb:qb + 64, 0:128], a["ones_row"][0:1, 0:64],
                               scr[0][0:1, 128:192].bitcast(BF16)[0:1, 0:128],
                               start=True, stop=True, tile_position=(0, qb))
                c.inc(ins, "tmm", 1)
                assert c.n("tmm") == tmmB(z, i, TI), (z, i)
    attn_done = c.n("vev")

    t_w0 = c.n("tmm")
    v_w0 = c.n("vev")
    wb2 = {0: c.n("wsA"), 1: c.n("wsB")}

    @c.block.gpsimd
    def _(g):
        for oc in range(DC):
            lane = "wsA" if oc % 2 == 0 else "wsB"
            c.w(g, "tmm", t_w0 if oc < 2 else t_w0 + 4 * (oc - 1))
            c.inc(g.dma_start(
                out=piece[oc % 2].rearrange("p (cc f) -> p cc f", cc=DC),
                in_=aps["A1WoT"].rearrange("(cc p) f -> p cc f", p=P)
                    [:, :, oc * 128:(oc + 1) * 128]), lane, 16)

    @c.block.tensor
    def _(t):
        for oc in range(DC):
            lane = "wsA" if oc % 2 == 0 else "wsB"
            c.w(t, lane, wb2[oc % 2] + (oc // 2) * 16 + 16)
            if oc == 0:
                c.w(t, "vev", attn_done)
            for ti in range(4):
                if oc * 4 + ti >= 4:
                    c.w(t, "vev", v_w0 + oc * 4 + ti - 3)
                for cc in range(DC):
                    ins = t.matmul(pb[ti % 4][:, 0:512],
                                   piece[oc % 2][:, cc * 128:(cc + 1) * 128],
                                   xn[:, cc * T + ti * 512: cc * T + (ti + 1) * 512],
                                   start=(cc == 0), stop=(cc == DC - 1))
                c.inc(ins, "tmm", 1)

    @c.block.vector
    def _(v):
        for oc in range(DC):
            for ti in range(4):
                c.w(v, "tmm", t_w0 + oc * 4 + ti + 1)
                rsl = a["res"][:, oc * T + ti * 512: oc * T + (ti + 1) * 512]
                c.inc(v.scalar_tensor_tensor(rsl, pb[ti % 4][:, 0:512],
                                             bq[:, 24 + oc:25 + oc], rsl,
                                             ALU.add, ALU.add), "vev", 1)


def build_l1attn_a_launch():
    nc = bass.Bass("TRN2", target_bir_lowering=False)
    gap = {}
    gap["res_in"] = nc.dram_tensor("res_in", [D, T], F32, kind="ExternalInput").ap()
    gap["identB"] = nc.dram_tensor("identB", [P, P], BF16, kind="ExternalInput").ap()
    gap["ones_colB"] = nc.dram_tensor("ones_colB", [P, 1], BF16, kind="ExternalInput").ap()
    gap["ones_rowB"] = nc.dram_tensor("ones_rowB", [1, P], BF16, kind="ExternalInput").ap()
    gap["cmaskB"] = nc.dram_tensor("cmaskB", [P, P], BF16, kind="ExternalInput").ap()
    for nm in ("A1WqT", "A1WkT", "A1WvT", "A1WoT"):
        gap[nm] = nc.dram_tensor(nm, [D, D], F32, kind="ExternalInput").ap()
    for nm in ("A1bq", "A1bk", "A1bv", "A1bo"):
        gap[nm] = nc.dram_tensor(nm, [P, DC], F32, kind="ExternalInput").ap()
    gap["selcum"] = nc.dram_tensor("selcum", [P, NCORES], F32, kind="ExternalInput").ap()
    gap["res_out"] = nc.dram_tensor("res_out", [D, T], F32, kind="ExternalOutput").ap()
    cc_in = nc.dram_tensor("cca_in", [P, 2080], BF16).ap()
    cc_out = nc.dram_tensor("cca_out", [P * NCORES, 2080], BF16).ap()

    a = alloc_common(nc)
    a["selcol"] = nc.alloc_sbuf_tensor("s_selcol", [P, NCORES], F32)
    a["cmask"] = nc.alloc_sbuf_tensor("s_cmask", [P, P], BF16)
    a["onesf"] = nc.alloc_sbuf_tensor("onesf", [P, 1], F32)
    with nc.Block() as block:
        c = Ctx(nc, block)
        c.gap = gap
        emit_load_consts(c, a)

        @c.block.sync
        def _(sync):
            c.inc(sync.dma_start(out=a["cmask"][:], in_=gap["cmaskB"]), "sld", 16)

        @c.block.vector
        def _(v):
            mk = c.inc(v.memset(a["onesf"][:], 1.0), "vsq", 1)
            c.w(v, "vsq", mk)

        emit_load_res(c, a, gap["res_in"])
        emit_ln(c, a, a["b2"][:], pre=[("sld", c.n("sld"))])
        emit_l1attn_a(c, a, gap, cc_in, cc_out)
        emit_store_res(c, a, gap["res_out"], [("vev", c.n("vev"))])
    return nc




def emit_l1attn_b(c, a, aps):
    b1 = a["b1"][:]; b2 = a["b2"][:]; f1 = a["f1"][:]; b3 = a["b3"][:]
    xn = b2
    qfb = b1[:, 0:16384]
    kfb = f1[:, 0:8192].bitcast(BF16)
    t1b = kfb
    piece = [f1[:, 8192:8704].bitcast(BF16), f1[:, 8704:9216].bitcast(BF16)]
    t2b = f1[:, 10240:11264].bitcast(BF16)
    logits = f1[:, 9216:10240].bitcast(BF16)
    gtt = f1[:, 10240:10256]
    gtb = f1[:, 10256:10264].bitcast(BF16)
    Eslot = [f1[:, 8192 + 256 * i: 8448 + 256 * i].bitcast(BF16) for i in range(4)]
    prod = [f1[:, 9216:9472].bitcast(BF16), f1[:, 9472:9728].bitcast(BF16)]
    gstage = f1[:, 9728:9984].bitcast(BF16)
    geff = gstage
    denb = f1[:, 9984:10240].bitcast(BF16)
    oacc = f1[:, 10240:10752]
    otmp = f1[:, 10752:11264]
    pb = a["pb"]; pt = a["pt"]
    bq = a["biasq"]

    def vch(cc):
        if cc < 4:
            return b1[:, 16384 + cc * 2048: 16384 + (cc + 1) * 2048]
        return b3[:, (cc - 4) * 2048: (cc - 3) * 2048]

    @c.block.sync
    def _(sync):
        c.inc(sync.dma_start(out=bq[:, 0:DC], in_=aps["B1bq"]), "sl2", 16)
        c.inc(sync.dma_start(out=bq[:, DC:16], in_=aps["B1bk"]), "sl2", 16)
        c.inc(sync.dma_start(out=bq[:, 16:24], in_=aps["B1bv"]), "sl2", 16)
        c.inc(sync.dma_start(out=bq[:, 24:32], in_=aps["B1bo"]), "sl2", 16)
        c.inc(sync.dma_start(out=bq[:, 32:40], in_=aps["B1b1"]), "sl2", 16)
        c.inc(sync.dma_start(out=bq[0:64, 40:41], in_=aps["B1b2c"]), "sl2", 16)
        c.inc(sync.dma_start(out=a["hsel"][:], in_=aps["hselB"]), "sl2", 16)
        c.inc(sync.dma_start(out=a["hbsel"][:], in_=aps["hbselB"]), "sl2", 16)
    sl2_b = c.n("sl2")

    @c.block.gpsimd
    def _(g):
        c.inc(g.dma_start(out=a["sigT"][:], in_=aps["B1sigT"]), "sl2x", 16)
    sl2x_b = c.n("sl2x")

    @c.block.vector
    def _(v):
        c.w(v, "sl2", sl2_b)
        mk = c.inc(v.tensor_scalar(bq[:, 41:49], bq[:, 0:DC], 0.125, None,
                                   ALU.mult), "vsq", 1)
        c.w(v, "vsq", mk)
        c.inc(v.engine_nop(), "vev", 1)
    bias_v = c.n("vev")

    ev_marks = {}

    def projx(w_ap, dst_fn, bcol0, scale=1.0, first=False, act=None):
        for oc in range(DC):
            g2 = oc % 2
            lane = "wsA" if g2 == 0 else "wsB"
            t_base = c.n("tmm")

            @c.block.gpsimd
            def _(g, oc=oc, lane=lane, g2=g2, t_base=t_base, w_ap=w_ap):
                c.w(g, "tmm", t_base)
                c.inc(g.dma_start(
                    out=piece[g2].rearrange("p (cc f) -> p cc f", cc=DC),
                    in_=w_ap.rearrange("(cc p) f -> p cc f", p=P)
                        [:, :, oc * 128:(oc + 1) * 128]), lane, 16)
            wmark = c.n(lane)
            mm_marks = []

            @c.block.tensor
            def _(t, oc=oc, lane=lane, g2=g2, wmark=wmark, first=first,
                  mm_marks=mm_marks):
                c.w(t, lane, wmark)
                if oc == 0 and first:
                    c.w(t, "vev", bias_v)
                for ti in range(4):
                    bank = ti % 4
                    if bank in ev_marks:
                        c.w(t, "sev", ev_marks[bank])
                    for cc in range(DC):
                        ins = t.matmul(pb[bank][:, 0:512],
                                       piece[g2][:, cc * 128:(cc + 1) * 128],
                                       xn[:, cc * T + ti * 512: cc * T + (ti + 1) * 512],
                                       start=(cc == 0), stop=(cc == DC - 1))
                    mm_marks.append(c.inc(ins, "tmm", 1))

            @c.block.scalar
            def _(s, oc=oc, bcol0=bcol0, scale=scale, act=act,
                  mm_marks=mm_marks):
                c.w(s, "sl2", sl2_b)
                for ti in range(4):
                    c.w(s, "tmm", mm_marks[ti])
                    m = c.inc(s.activation(
                        dst_fn(oc)[:, ti * 512:(ti + 1) * 512],
                        pb[ti % 4][:, 0:512],
                        act if act is not None else AF.Identity,
                        bias=bq[:, bcol0 + oc:bcol0 + oc + 1], scale=scale),
                        "sev", 1)
                    ev_marks[ti % 4] = m

    projx(aps["B1W1T"], lambda oc: t1b[:, oc * T:(oc + 1) * T], 32,
          first=True, act=AF.Relu)

    t_t2 = c.n("tmm")

    @c.block.gpsimd
    def _(g):
        c.w(g, "tmm", t_t2)
        c.inc(g.dma_start(
            out=piece[1][:, 0:512].rearrange("p (cc f) -> p cc f", cc=DC),
            in_=aps["B1W2T"].rearrange("(cc p) f -> p cc f", p=P)), "wsA", 16)
    w2m = c.n("wsA")
    t2_marks = []

    @c.block.tensor
    def _(t):
        c.w(t, "wsA", w2m)
        for ti in range(4):
            if (ti % 4) in ev_marks:
                c.w(t, "sev", ev_marks[ti % 4])
            for cc in range(DC):
                ins = t.matmul(pb[ti % 4][0:64, 0:512],
                               piece[1][:, cc * 64:(cc + 1) * 64],
                               t1b[:, cc * T + ti * 512: cc * T + (ti + 1) * 512],
                               start=(cc == 0), stop=(cc == DC - 1))
            t2_marks.append(c.inc(ins, "tmm", 1))

    t2e_marks = []

    @c.block.scalar
    def _(s):
        for ti in range(4):
            c.w(s, "tmm", t2_marks[ti])
            m = c.inc(s.activation(t2b[0:64, ti * 512:(ti + 1) * 512],
                                   pb[ti % 4][0:64, 0:512], AF.Identity,
                                   bias=bq[0:64, 40:41]), "sev", 1)
            t2e_marks.append(m)
            ev_marks[ti % 4] = m

    lg_marks = []

    @c.block.tensor
    def _(t):
        c.w(t, "sl2x", sl2x_b)
        for ti in range(4):
            c.w(t, "sev", t2e_marks[ti])
            ins = t.matmul(pb[ti % 4][0:16, 0:512], a["sigT"][:],
                           t2b[0:64, ti * 512:(ti + 1) * 512],
                           start=True, stop=True)
            lg_marks.append(c.inc(ins, "tmm", 1))

    lge_marks = []

    @c.block.vector
    def _(v):
        for ti in range(4):
            c.w(v, "tmm", lg_marks[ti])
            lge_marks.append(c.inc(v.tensor_copy(
                logits[0:16, ti * 512:(ti + 1) * 512],
                pb[ti % 4][0:16, 0:512]), "vev", 1))
    lg_v = c.n("vev")

    vG = c.n("vev")
    sG = c.n("sev")
    tG = c.n("tmm")

    @c.block.tensor
    def _(t):
        c.w(t, "vev", lg_v)
        for j in range(NTOK):
            if j >= 2:
                c.w(t, "vev", vG + 3 * (j - 1))
            ins = t.transpose(pt[j % 2][0:128, 0:16],
                              logits[0:16, j * P:(j + 1) * P],
                              a["ident"][0:16, 0:16])
            c.inc(ins, "tmm", 1)

    @c.block.scalar
    def _(s):
        for j in range(NTOK):
            c.w(s, "vev", vG + 3 * j + 2)
            c.inc(s.activation(gtb[:, 0:16], gtt[:, 0:16], AF.Exp,
                               bias=a["cols"][:, 3:4],
                               accum_out=a["cols"][:, 4:5]), "sev", 1)

    @c.block.vector
    def _(v):
        for j in range(NTOK):
            c.w(v, "tmm", tG + j + 1)
            mk = c.inc(v.tensor_copy(gtt[:, 0:16], pt[j % 2][0:128, 0:16]),
                       "vev", 1)
            c.w(v, "vev", mk)
            mk = c.inc(v.reduce_max(a["cols"][:, 2:3], gtt[:, 0:16], AX.X),
                       "vsq", 1)
            c.w(v, "vsq", mk)
            mk = c.inc(v.tensor_scalar(a["cols"][:, 3:4], a["cols"][:, 2:3],
                                       -1.0, None, ALU.mult), "vev", 1)
            c.w(v, "vev", mk)
            c.w(v, "sev", sG + j + 1)
            mk = c.inc(v.reciprocal(a["cols"][:, 5:6], a["cols"][:, 4:5]),
                       "vsq", 1)
            c.w(v, "vsq", mk)
            c.inc(v.tensor_scalar(a["gtm"][:, j * 16:(j + 1) * 16],
                                  gtb[:, 0:16], a["cols"][:, 5:6], None,
                                  ALU.mult), "vev", 1)
    gate_v = c.n("vev")

    projx(aps["B1WqT"], lambda oc: qfb[:, oc * T:(oc + 1) * T], 41, scale=0.125)
    projx(aps["B1WkT"], lambda oc: kfb[:, oc * T:(oc + 1) * T], DC)
    projx(aps["B1WvT"], vch, 16)
    proj_s = c.n("sev")

    tS = c.n("tmm")
    vS = c.n("vev")
    sS = c.n("sev")
    TT_ = 32 + 4 + 40
    VT_ = 32 + 1 + 5 + 64

    def t_sc(tt, ss, cc):
        return tS + tt * TT_ + ss * 8 + cc + 1

    def t_gt(tt, j4):
        return tS + tt * TT_ + 32 + j4 + 1

    def t_ob(tt, cc, k):
        return tS + tt * TT_ + 36 + cc * 5 + k + 1

    def v_pr(tt, ss, cc):
        return vS + tt * VT_ + ss * 8 + cc + 1

    def v_dn(tt):
        return vS + tt * VT_ + 33

    def v_ge(tt, j4):
        return vS + tt * VT_ + 34 + j4

    def v_os(tt, cc, k):
        return vS + tt * VT_ + 38 + cc * 8 + k + 1

    def s_ex(tt, ss):
        return sS + tt * 4 + ss + 1

    @c.block.vector
    def _(v):
        c.w(v, "sev", proj_s)
        for tt in range(B):
            for ss in range(B):
                for cc in range(DC):
                    gp = ss * 8 + cc
                    if gp >= 2:
                        c.w(v, "tmm", t_sc(tt, (gp - 2) // 8, (gp - 2) % 8))
                    elif tt > 0:
                        c.w(v, "tmm", tS + tt * TT_)
                    c.inc(v.tensor_mul(
                        prod[cc % 2][:, 0:512],
                        qfb[:, cc * T + tt * SL: cc * T + (tt + 1) * SL],
                        kfb[:, cc * T + ss * SL: cc * T + (ss + 1) * SL]),
                        "vev", 1)
            c.w(v, "sev", s_ex(tt, 3))
            mk = c.inc(v.tensor_add(denb[0:16, 0:512], Eslot[0][0:16, 0:512],
                                    Eslot[1][0:16, 0:512]), "vsq", 1)
            c.w(v, "vsq", mk)
            mk = c.inc(v.tensor_add(denb[0:16, 0:512], denb[0:16, 0:512],
                                    Eslot[2][0:16, 0:512]), "vsq", 1)
            c.w(v, "vsq", mk)
            mk = c.inc(v.tensor_add(denb[0:16, 0:512], denb[0:16, 0:512],
                                    Eslot[3][0:16, 0:512]), "vsq", 1)
            c.w(v, "vsq", mk)
            c.inc(v.reciprocal(otmp[0:16, 0:512], denb[0:16, 0:512]), "vev", 1)
            for j4 in range(4):
                c.w(v, "tmm", t_gt(tt, j4))
                c.inc(v.tensor_copy(gstage[0:16, j4 * 128:(j4 + 1) * 128],
                                    pt[j4 % 2][0:128, 0:16][0:16, 0:128]
                                    if False else
                                    pt[j4 % 2][0:16, 0:128]), "vev", 1)
            c.w(v, "vev", v_ge(tt, 3))
            mk = c.inc(v.tensor_mul(geff[0:16, 0:512], gstage[0:16, 0:512],
                                    otmp[0:16, 0:512]), "vev", 1)
            c.w(v, "vev", mk)
            for cc in range(DC):
                for ss in range(B):
                    c.w(v, "tmm", t_ob(tt, cc, ss))
                    if ss == 0:
                        mk = c.inc(v.tensor_mul(oacc[:, 0:512],
                                                pb[1][:, 0:512],
                                                vch(cc)[:, ss * SL:(ss + 1) * SL]),
                                   "vsq", 1)
                        c.w(v, "vsq", mk)
                        c.inc(v.engine_nop(), "vev", 1)
                    else:
                        mk = c.inc(v.tensor_mul(otmp[:, 0:512],
                                                pb[1][:, 0:512],
                                                vch(cc)[:, ss * SL:(ss + 1) * SL]),
                                   "vsq", 1)
                        c.w(v, "vsq", mk)
                        mk2 = c.inc(v.tensor_add(oacc[:, 0:512], oacc[:, 0:512],
                                                 otmp[:, 0:512]), "vsq", 1)
                        c.w(v, "vsq", mk2)
                        c.inc(v.engine_nop(), "vev", 2)
                c.w(v, "tmm", t_ob(tt, cc, 4))
                mk = c.inc(v.tensor_mul(otmp[:, 0:512], oacc[:, 0:512],
                                        pb[2][:, 0:512]), "vsq", 1)
                c.w(v, "vsq", mk)
                c.inc(v.tensor_copy(
                    xn[:, cc * T + tt * SL: cc * T + (tt + 1) * SL],
                    otmp[:, 0:512]), "vev", 1)
            assert c.n("vev") == vS + (tt + 1) * VT_, (tt, c.n("vev"))

    @c.block.scalar
    def _(s):
        for tt in range(B):
            for ss in range(B):
                c.w(s, "tmm", t_sc(tt, ss, 7))
                if tt > 0:
                    c.w(s, "vev", vS + tt * VT_)
                c.inc(s.activation(Eslot[ss][0:16, 0:512], pb[0][0:16, 0:512],
                                   AF.Exp), "sev", 1)

    @c.block.tensor
    def _(t):
        for tt in range(B):
            for ss in range(B):
                for cc in range(DC):
                    c.w(t, "vev", v_pr(tt, ss, cc))
                    if cc == 0 and tt * 4 + ss >= 1:
                        prev = tt * 4 + ss - 1
                        c.w(t, "sev", s_ex(prev // 4, prev % 4))
                    ins = t.matmul(pb[0][0:16, 0:512],
                                   a["hsel"][:, cc * 16:(cc + 1) * 16],
                                   prod[cc % 2][:, 0:512],
                                   start=(cc == 0), stop=(cc == DC - 1))
                    c.inc(ins, "tmm", 1)
            for j4 in range(4):
                c.w(t, "vev", v_dn(tt))
                if j4 >= 2:
                    c.w(t, "vev", v_ge(tt, j4 - 2))
                ins = t.transpose(pt[j4 % 2][0:16, 0:128],
                                  a["gtm"][:, (tt * 4 + j4) * 16:
                                           (tt * 4 + j4 + 1) * 16],
                                  a["ident"][:])
                c.inc(ins, "tmm", 1)
            for cc in range(DC):
                for ss in range(B):
                    if cc == 0 and ss == 0:
                        c.w(t, "vev", v_ge(tt, 3) + 1)
                    if ss > 0:
                        c.w(t, "vev", v_os(tt, cc, [0, 2, 4][ss - 1]))
                    elif cc > 0:
                        c.w(t, "vev", v_os(tt, cc - 1, 7))
                    ins = t.matmul(pb[1][:, 0:512],
                                   a["hbsel"][0:16, cc * 128:(cc + 1) * 128],
                                   Eslot[ss][0:16, 0:512], start=True, stop=True)
                    c.inc(ins, "tmm", 1)
                if cc > 0:
                    c.w(t, "vev", v_os(tt, cc - 1, 7))
                ins = t.matmul(pb[2][:, 0:512],
                               a["hbsel"][0:16, cc * 128:(cc + 1) * 128],
                               geff[0:16, 0:512], start=True, stop=True)
                c.inc(ins, "tmm", 1)
            assert c.n("tmm") == tS + (tt + 1) * TT_, (tt, c.n("tmm"))
    attn_v = c.n("vev")

    t_w0 = c.n("tmm")
    v_w0 = c.n("vev")
    wb2 = {0: c.n("wsA"), 1: c.n("wsB")}

    @c.block.gpsimd
    def _(g):
        for oc in range(DC):
            lane = "wsA" if oc % 2 == 0 else "wsB"
            c.w(g, "tmm", t_w0 if oc < 2 else t_w0 + 4 * (oc - 1))
            c.inc(g.dma_start(
                out=piece[oc % 2].rearrange("p (cc f) -> p cc f", cc=DC),
                in_=aps["B1WoT"].rearrange("(cc p) f -> p cc f", p=P)
                    [:, :, oc * 128:(oc + 1) * 128]), lane, 16)

    @c.block.tensor
    def _(t):
        for oc in range(DC):
            lane = "wsA" if oc % 2 == 0 else "wsB"
            c.w(t, lane, wb2[oc % 2] + (oc // 2) * 16 + 16)
            if oc == 0:
                c.w(t, "vev", attn_v)
            for ti in range(4):
                if oc * 4 + ti >= 4:
                    c.w(t, "vev", v_w0 + oc * 4 + ti - 3)
                for cc in range(DC):
                    ins = t.matmul(pb[ti % 4][:, 0:512],
                                   piece[oc % 2][:, cc * 128:(cc + 1) * 128],
                                   xn[:, cc * T + ti * 512: cc * T + (ti + 1) * 512],
                                   start=(cc == 0), stop=(cc == DC - 1))
                c.inc(ins, "tmm", 1)

    @c.block.vector
    def _(v):
        for oc in range(DC):
            for ti in range(4):
                c.w(v, "tmm", t_w0 + oc * 4 + ti + 1)
                rsl = a["res"][:, oc * T + ti * 512: oc * T + (ti + 1) * 512]
                c.inc(v.scalar_tensor_tensor(rsl, pb[ti % 4][:, 0:512],
                                             bq[:, 24 + oc:25 + oc], rsl,
                                             ALU.add, ALU.add), "vev", 1)


def build_l1attn_b_launch():
    nc = bass.Bass("TRN2", target_bir_lowering=False)
    gap = {}
    gap["res_in"] = nc.dram_tensor("res_in", [D, T], F32, kind="ExternalInput").ap()
    gap["identB"] = nc.dram_tensor("identB", [P, P], BF16, kind="ExternalInput").ap()
    gap["ones_colB"] = nc.dram_tensor("ones_colB", [P, 1], BF16, kind="ExternalInput").ap()
    gap["ones_rowB"] = nc.dram_tensor("ones_rowB", [1, P], BF16, kind="ExternalInput").ap()
    for nm in ("B1WqT", "B1WkT", "B1WvT", "B1WoT", "B1W1T"):
        gap[nm] = nc.dram_tensor(nm, [D, D if nm != "B1W2T" else NE], F32,
                                 kind="ExternalInput").ap()
    gap["B1W2T"] = nc.dram_tensor("B1W2T", [D, NE], F32, kind="ExternalInput").ap()
    for nm in ("B1bq", "B1bk", "B1bv", "B1bo", "B1b1"):
        gap[nm] = nc.dram_tensor(nm, [P, DC], F32, kind="ExternalInput").ap()
    gap["B1b2c"] = nc.dram_tensor("B1b2c", [NE, 1], F32, kind="ExternalInput").ap()
    gap["B1sigT"] = nc.dram_tensor("B1sigT", [NE, NH], F32, kind="ExternalInput").ap()
    gap["hselB"] = nc.dram_tensor("hselB", [P, DC * NH], BF16, kind="ExternalInput").ap()
    gap["hbselB"] = nc.dram_tensor("hbselB", [NH, DC * P], BF16, kind="ExternalInput").ap()
    gap["res_out"] = nc.dram_tensor("res_out", [D, T], F32, kind="ExternalOutput").ap()

    a = alloc_common(nc)
    a["sigT"] = nc.alloc_sbuf_tensor("s_sigT", [NE, NH], BF16)
    a["hsel"] = nc.alloc_sbuf_tensor("s_hsel", [P, DC * NH], BF16)
    a["hbsel"] = nc.alloc_sbuf_tensor("s_hbsel", [NH, DC * P], BF16)
    a["gtm"] = nc.alloc_sbuf_tensor("s_gtm", [P, NTOK * NH], BF16)
    with nc.Block() as block:
        c = Ctx(nc, block)
        c.gap = gap
        emit_load_consts(c, a)
        emit_load_res(c, a, gap["res_in"])
        emit_ln(c, a, a["b2"][:], pre=[("sld", c.n("sld"))])
        emit_l1attn_b(c, a, gap)
        emit_store_res(c, a, gap["res_out"], [("vev", c.n("vev"))])
    return nc


def build_moe_launch(phase=3):
    nc = bass.Bass("TRN2", target_bir_lowering=False)
    gap = {}
    gap["res_in"] = nc.dram_tensor("res_in", [D, T], F32, kind="ExternalInput").ap()
    gap["identB"] = nc.dram_tensor("identB", [P, P], BF16, kind="ExternalInput").ap()
    gap["ones_colB"] = nc.dram_tensor("ones_colB", [P, 1], BF16, kind="ExternalInput").ap()
    gap["ones_rowB"] = nc.dram_tensor("ones_rowB", [1, P], BF16, kind="ExternalInput").ap()
    gap["centT"] = nc.dram_tensor("centT", [D, NE], F32, kind="ExternalInput").ap()
    gap["ebias"] = nc.dram_tensor("ebias", [1, NE], F32, kind="ExternalInput").ap()
    gap["Wsu"] = nc.dram_tensor("Wsu", [D, FF], F32, kind="ExternalInput").ap()
    gap["Wsd"] = nc.dram_tensor("Wsd", [FF, D], F32, kind="ExternalInput").ap()
    gap["Wu"] = nc.dram_tensor("Wu", [NE, D, FF], F32, kind="ExternalInput").ap()
    gap["Wd"] = nc.dram_tensor("Wd", [NE, FF, D], F32, kind="ExternalInput").ap()
    gap["res_out"] = nc.dram_tensor("res_out", [D, T], F32, kind="ExternalOutput").ap()
    gap["dbg"] = nc.dram_tensor("dbg", [P, 4096], BF16, kind="ExternalOutput").ap()

    a = alloc_common(nc)
    with nc.Block() as block:
        c = Ctx(nc, block)
        c.gap = gap
        emit_load_consts(c, a)
        emit_load_res(c, a, gap["res_in"])
        sldv = c.n("sld")

        @c.block.vector
        def _(v):
            c.w(v, "sld", sldv)

        @c.block.scalar
        def _(s):
            c.w(s, "sld", sldv)

        emit_ln(c, a, a["b2"][:])
        if phase >= 2:
            emit_moe(c, a, gap["Wu"], gap["Wd"], gap["Wsu"], gap["Wsd"],
                     gap["centT"], gap["ebias"], router_only=(phase == 2))
        if phase == 1:
            @c.block.sync
            def _(sync):
                c.w(sync, "vev", c.n("vev"))
                c.inc(sync.dma_start(out=gap["dbg"], in_=a["b2"][:, 0:4096]),
                      "sst", 16)
        elif phase == 2:
            @c.block.sync
            def _(sync):
                c.w(sync, "vrt", c.n("vrt"))
                c.inc(sync.dma_start(out=gap["dbg"][0:NE, 0:T],
                                     in_=a["b1"][0:NE, 18432:18432 + T]),
                      "sst", 16)
        emit_store_res(c, a, gap["res_out"],
                       [("vdn", c.n("vdn")), ("vev", c.n("vev"))])
    return nc



def _run(nc, maps, core_ids):
    res = run_bass_kernel_spmd(nc, maps, core_ids=core_ids)
    return res.results


def kernel(**inputs):
    g, per_core = host_prep(inputs)
    ncores = NCORES
    cores = list(range(ncores))

    def sel_onehot(c):
        s = np.zeros((P, NCORES), np.float32)
        s[:, c] = 1.0
        return s

    def sel_cum(c):
        s = np.zeros((P, NCORES), np.float32)
        s[:, :c] = 1.0
        return s

    def sel_prev(c):
        s = np.zeros((P, NCORES), np.float32)
        if c > 0:
            s[:, c - 1] = 1.0
        return s

    consts = {k: g[k] for k in ("identB", "ones_colB", "ones_rowB")}
    res = [pc["res_in"] for pc in per_core]

    def launch(build_fn, keys, extra_fn=None):
        nonlocal res
        nc = build_fn()
        maps = []
        for ci in range(ncores):
            m = dict(consts)
            for k in keys:
                m[k] = g[k]
            m["res_in"] = res[ci]
            if extra_fn:
                m.update(extra_fn(ci))
            maps.append(m)
        out = _run(nc, maps, cores)
        res = [np.ascontiguousarray(np.asarray(out[ci]["res_out"], np.float32))
               for ci in range(ncores)]

    launch(build_attn_a_launch,
           ["pswapB", "Wqkv", "bqkv", "Wo0", "bo0"],
           lambda ci: {k: per_core[ci][k]
                       for k in ("x_ext", "ropeC", "ropeS", "swmaskB")})
    moe_keys = ["centT", "ebias", "Wsu", "Wsd", "Wu", "Wd"]
    launch(build_moe_launch, moe_keys)
    launch(build_attn_b_launch, ["WzT", "bz", "WhT", "bh"],
           lambda ci: {"selcol": sel_onehot(ci)})
    launch(build_moe_launch, moe_keys)
    launch(build_l1attn_a_launch,
           ["cmaskB", "A1WqT", "A1WkT", "A1WvT", "A1WoT",
            "A1bq", "A1bk", "A1bv", "A1bo"],
           lambda ci: {"selcum": sel_cum(ci)})
    ffn_keys = ["F1WrT", "F1WkT", "F1WvT", "F1mur", "F1muk"]
    launch(build_l1ffn_launch, ffn_keys,
           lambda ci: {"selprev": sel_prev(ci)})
    launch(build_l1attn_b_launch,
           ["B1WqT", "B1WkT", "B1WvT", "B1WoT", "B1W1T", "B1W2T",
            "B1bq", "B1bk", "B1bv", "B1bo", "B1b1", "B1b2c", "B1sigT",
            "hselB", "hbselB"])
    launch(build_l1ffn_launch, ffn_keys,
           lambda ci: {"selprev": sel_prev(ci)})

    out = np.zeros((D, B, S), np.float32)
    for ci in range(ncores):
        out[:, :, ci * SL:(ci + 1) * SL] = res[ci].reshape(D, B, SL)
    return np.ascontiguousarray(out.transpose(1, 2, 0))
